# version 18
# baseline (speedup 1.0000x reference)
import numpy as np
import concourse.bass as bass
import concourse.mybir as mybir
from concourse.bass_utils import run_bass_kernel_spmd
from contextlib import ExitStack

F32 = mybir.dt.float32
BF16 = mybir.dt.bfloat16
F16 = mybir.dt.float16
AF = mybir.ActivationFunctionType
ALU = mybir.AluOpType
AX = mybir.AxisListType

MAGIC = float(1.5 * 2.0 ** 23)
S_LEN = 2048
D = 1024
NT = 16
NCH = 8
NQC = 4
NPAIR = 8


class Prog:
    ENGS = ("sync", "scalar", "vector", "tensor", "gpsimd")

    def __init__(self, nc, stack, serialize=False):
        self.nc = nc
        self.stack = stack
        self.streams = {e: [] for e in self.ENGS}
        self.sems = {}
        self.counts = {}
        self.serialize = serialize

    def sem(self, name):
        self.sems[name] = self.stack.enter_context(self.nc.semaphore(name))
        self.counts[name] = 0
        return name

    def op(self, eng, fn, waits=(), inc=None, amount=1, tdrain=False, pub_drain=True):
        waits = list(waits)
        if self.serialize:
            skip = ("s_ld", "s_x", "sfin")
            waits = waits + [(s, self.counts[s]) for s in self.counts if s not in skip]
        if inc is not None:
            self.counts[inc] += amount
        self.streams[eng].append((waits, fn, inc, amount, tdrain, pub_drain))

    def val(self, s):
        return self.counts[s]

    def emit(self, block):
        for engname in self.ENGS:
            ops = self.streams[engname]
            if not ops:
                continue
            dec = getattr(block, engname)

            def _body(eng, _ops=ops, _drain=(engname in ("vector", "scalar"))):
                for waits, fn, inc, amount, tdrain, pub_drain in _ops:
                    for s, v in waits:
                        if v > 0:
                            eng.wait_ge(self.sems[s], v)
                    ins = fn(eng)
                    if inc is not None:
                        if (_drain and pub_drain) or tdrain:
                            eng.drain().then_inc(self.sems[inc], amount)
                        else:
                            ins.then_inc(self.sems[inc], amount)

            dec(_body)


def _r3(ap, c, lo=None, hi=None):
    v = ap.rearrange("p (c n) -> p c n", c=c)
    if lo is not None:
        v = v[:, :, lo:hi]
    return v


def build(debug=(), serialize=False):
    nc = bass.Bass("TRN2", target_bir_lowering=False)

    x_h = nc.dram_tensor("x", [S_LEN, D], F32, kind="ExternalInput").ap()
    wq_h = nc.dram_tensor("wq", [D, D], F32, kind="ExternalInput").ap()
    wk_h = nc.dram_tensor("wk", [D, D], F32, kind="ExternalInput").ap()
    wv_h = nc.dram_tensor("wv", [D, D], F32, kind="ExternalInput").ap()
    wo_h = nc.dram_tensor("wo", [D, D], F32, kind="ExternalInput").ap()
    wf_h = nc.dram_tensor("wf", [D, D], F32, kind="ExternalInput").ap()
    bq_h = nc.dram_tensor("bq", [D], F32, kind="ExternalInput").ap()
    bk_h = nc.dram_tensor("bk", [D], F32, kind="ExternalInput").ap()
    bv_h = nc.dram_tensor("bv", [D], F32, kind="ExternalInput").ap()
    bo_h = nc.dram_tensor("bo", [D], F32, kind="ExternalInput").ap()
    bf_h = nc.dram_tensor("bfb", [D], F32, kind="ExternalInput").ap()
    cos_h = nc.dram_tensor("cost", [S_LEN, 32], F32, kind="ExternalInput").ap()
    sin_h = nc.dram_tensor("sint", [S_LEN, 32], F32, kind="ExternalInput").ap()
    out_h = nc.dram_tensor("out", [S_LEN, D], F32, kind="ExternalOutput").ap()

    dbg_h = {}
    def dbg_tensor(name, shape):
        if name in debug:
            dbg_h[name] = nc.dram_tensor("d_" + name, shape, F32, kind="ExternalOutput").ap()

    dbg_tensor("ln1f", [128, NCH * S_LEN // 2])
    dbg_tensor("qf", [128, NCH * S_LEN // 2])
    dbg_tensor("kf", [128, NCH * S_LEN // 2])
    dbg_tensor("vtm", [128, (NT * 1040 + 64) // 2])
    dbg_tensor("wbfB", [128, NCH * D // 2])
    dbg_tensor("wbfA2", [128, NCH * D // 2])
    dbg_tensor("negcb", [128, 1])
    dbg_tensor("of", [128, NCH * S_LEN // 2])
    dbg_tensor("rs", [8, NQC * 1024])
    dbg_tensor("x2", [S_LEN, D])
    dbg_tensor("psb0", [128, 512])
    dbg_tensor("qsqr", [128, 16])
    dbg_tensor("cbt2", [16, 1])
    dbg_tensor("cbt5pre", [1, 1])
    dbg_tensor("ksqr", [128, 16])

    with ExitStack() as st:
        sb = lambda n, s, d: st.enter_context(nc.sbuf_tensor(n, s, d))
        p = Prog(nc, st, serialize=serialize)

        P0 = st.enter_context(nc.psum_tensor("P0", [128, 1024], F32))
        P1 = st.enter_context(nc.psum_tensor("P1", [128, 1024], F32))
        P2 = st.enter_context(nc.psum_tensor("P2", [128, 1024], F32))
        P3 = st.enter_context(nc.psum_tensor("P3", [128, 1024], F32))
        pq = [P0[:], P1[:]]
        ptT = [P2[:, 0:512].bitcast(BF16), P2[:, 512:1024].bitcast(BF16)]
        po_lo = [P2[:, 0:512], P3[:, 0:512]]
        po_hi = [P2[:, 512:1024], P3[:, 512:1024]]
        pf = P3[:]

        ident_bf = sb("ident_bf", [128, 128], BF16)
        ident_f = sb("ident_f", [128, 128], F32)
        tri2 = sb("tri2", [128, 256], BF16)
        mask8 = sb("mask8", [8, 1024], BF16)
        ones_bf = sb("ones_bf", [128, 1], BF16)
        ones_row_f = sb("ones_row_f", [1, 128], F32)
        cmag = sb("cmag", [128, 1], F32)
        cnmag = sb("cnmag", [128, 1], F32)
        cone = sb("cone", [128, 1], F32)
        cp2 = sb("cp2", [128, 8], F32)
        czero = cp2[:, 0:1]
        cm1 = cp2[:, 1:2]
        cinvd = cp2[:, 2:3]
        cninvd = cp2[:, 3:4]
        cepsd = cp2[:, 4:5]
        cnep8 = cp2[:, 5:6]
        ci127 = cp2[:, 6:7]
        cep8d = cp2[:, 7:8]
        cos_c = sb("cos_c", [128, 512], F32)
        sin_c = sb("sin_c", [128, 512], F32)
        bias0 = sb("bias0", [128, D], F32)
        bias1 = sb("bias1", [128, D], F32)

        xt = [sb(f"xt{i}", [128, D], F32) for i in range(4)]
        cbtmp = xt[0][0:16, 0:128]
        cbt2 = xt[0][0:16, 128:129]
        cbt4 = xt[0][0:1, 136:152]
        cbt5 = xt[0][0:1, 160:161]
        sc1 = sb("sc1", [128, D], F32)
        tmpA = sb("tmpA", [128, 512], F32)

        stats = {k: sb("st_" + k, [128, NT], F32)
                 for k in ("ssq", "ssm", "m", "rstd", "nmr", "am", "xs", "rxs")}
        wsc = sb("wsc", [128, 8], F32)
        wrxs = sb("wrxs", [128, 8], F32)
        qsqr = sb("qsqr", [128, 16], F32)
        ksqr = sb("ksqr", [128, 16], F32)
        nrm_t = sb("nrm_t", [128, 16], F32)
        negCb = sb("negCb", [128, 1], F32)
        wsr = sb("wsr", [8, 128], F32)

        wland = sb("wland", [128, D], F32)
        wbfA = sb("wbfA", [128, NCH * D], BF16)
        wbfB = sb("wbfB", [128, NCH * D], BF16)
        P_sb = [sb(f"psb{i}", [128, 1024], BF16) for i in range(2)]
        rstage = None

        featA = sb("featA", [128, NCH * S_LEN], BF16)
        vtm = sb("vtm", [128, NT * 1040 + 64], BF16)
        wsb = vtm[:, 0:2048].bitcast(F32)
        sc2 = vtm[:, 2048:4096].bitcast(F32)
        featQ = sb("featQ", [128, NCH * S_LEN], BF16)
        featK = sb("featK", [128, NCH * S_LEN], BF16)

        rs_all = wbfA[0:8, 0:8192].bitcast(F32)
        rs_hi = featK[0:8, 8192:12288]
        rs_lo = featK[0:8, 12288:16384]
        rs_tmp = featQ[0:8, 0:8192].bitcast(F32)
        mask8f = featQ[:, 2048:4096].bitcast(F32)[0:8, :]

        rb_sb = [featQ[:, 5120:6144].bitcast(F32), featQ[:, 6144:7168].bitcast(F32)]
        wfsc = featQ[:, 7168:9216].bitcast(F32)
        wqTf = featK[:, 0:8192]
        x2g = featQ[:, 0:8192].bitcast(F32)
        x2g4 = [x2g[:, i * 1024:(i + 1) * 1024] for i in range(4)]
        rdbuf = featQ[:, 8192:10240].bitcast(F32)
        t2buf = featQ[:, 10240:12288].bitcast(F32)
        xqT = featQ[:, 12288:13312]
        obuf = featK[:, 8192:16384].bitcast(F32)
        obuf4 = [obuf[:, i * 1024:(i + 1) * 1024] for i in range(4)]
        sc2g = vtm[:, 2048:10240].bitcast(F32)
        sc2g4 = [sc2g[:, i * 1024:(i + 1) * 1024] for i in range(4)]
        dum = vtm[:, 10240:12288].bitcast(F32)

        s_ld = p.sem("s_ld")
        s_x = p.sem("s_x")
        g_set = p.sem("g_set")
        v_set = p.sem("v_set")
        sa = p.sem("sa")
        sv = p.sem("sv")
        stt = p.sem("stt")
        smm = p.sem("smm")
        sexp = p.sem("sexp")
        smsk = p.sem("smsk")
        spv = p.sem("spv")
        swc = p.sem("swc")
        sfin = p.sem("sfin")
        sdbg = p.sem("sdbg")
        s_rg = p.sem("s_rg")
        s_g = p.sem("s_g")

        def dma(eng, out, in_):
            return eng.dma_start(out=out, in_=in_)

        p.op("sync", lambda e: dma(e, cos_c[:].rearrange("p (t j) -> p t j", t=NT),
                                   cos_h.rearrange("(t p) j -> p t j", p=128)), inc=s_ld, amount=16)
        p.op("sync", lambda e: dma(e, sin_c[:].rearrange("p (t j) -> p t j", t=NT),
                                   sin_h.rearrange("(t p) j -> p t j", p=128)), inc=s_ld, amount=16)
        p.op("sync", lambda e: dma(e, bias0[:], bq_h[None, :].partition_broadcast(128)), inc=s_ld, amount=16)
        p.op("sync", lambda e: dma(e, bias1[:], bk_h[None, :].partition_broadcast(128)), inc=s_ld, amount=16)
        setup_ld = p.val(s_ld)

        p.op("gpsimd", lambda e: e.memset(ident_bf[:], 1.0))
        p.op("gpsimd", lambda e: e.affine_select(ident_bf[:], ident_bf[:], pattern=[[-1, 128]],
                                                 compare_op=ALU.is_equal, fill=0.0, base=0,
                                                 channel_multiplier=1))
        p.op("gpsimd", lambda e: e.memset(ident_f[:], 1.0))
        p.op("gpsimd", lambda e: e.affine_select(ident_f[:], ident_f[:], pattern=[[-1, 128]],
                                                 compare_op=ALU.is_equal, fill=0.0, base=0,
                                                 channel_multiplier=1))
        p.op("gpsimd", lambda e: e.memset(tri2[:], 1.0))
        p.op("gpsimd", lambda e: e.affine_select(tri2[:], tri2[:], pattern=[[0, 2], [1, 128]],
                                                 compare_op=ALU.is_ge, fill=0.0, base=0,
                                                 channel_multiplier=-1))
        def mask_init(e, tgt):
            e.memset(tgt, 1.0)
            e.affine_select(tgt, tgt, pattern=[[1, 1024]], compare_op=ALU.is_ge,
                            fill=0.0, base=0, channel_multiplier=-128)
            return e.affine_select(tgt, tgt, pattern=[[-1, 1024]], compare_op=ALU.is_ge,
                                   fill=0.0, base=127, channel_multiplier=128)
        p.op("gpsimd", lambda e: mask_init(e, mask8[:]))
        p.op("gpsimd", lambda e: e.memset(ones_bf[:], 1.0), inc=g_set)

        p.op("vector", lambda e: e.memset(ones_row_f[:], 1.0))
        p.op("vector", lambda e: e.memset(cmag[:], MAGIC))
        p.op("vector", lambda e: e.memset(cnmag[:], -MAGIC))
        p.op("vector", lambda e: e.memset(czero, 0.0))
        p.op("vector", lambda e: e.memset(cm1, -1.0))
        p.op("vector", lambda e: e.memset(cinvd, 1.0 / D))
        p.op("vector", lambda e: e.memset(cninvd, -1.0 / D))
        p.op("vector", lambda e: e.memset(cepsd, 1e-5))
        p.op("vector", lambda e: e.memset(cnep8, -1e-8))
        p.op("vector", lambda e: e.memset(ci127, 1.0 / 127.0))
        p.op("vector", lambda e: e.memset(cep8d, 1e-8 / 127.0))
        p.op("vector", lambda e: e.memset(cone[:], 1.0), inc=v_set)

        wland_free = [0]
        cast_done = {}

        def w_dma(w_ap, c):
            def fn(e, w_ap=w_ap, c=c):
                return dma(e, wland[:], w_ap.rearrange("(c p) d -> c p d", p=128)[c])
            return fn

        def load_weight(name, w_ap, dest, first_waits=()):
            for c in range(NCH):
                p.op("sync", w_dma(w_ap, c), waits=[(swc, wland_free[0])], inc=s_ld, amount=16)
                def cast_fn(e, c=c, dest=dest):
                    return e.tensor_copy(out=dest[:, c * D:(c + 1) * D], in_=wland[:])
                cw = [(s_ld, p.val(s_ld))] + (list(first_waits) if c == 0 else [])
                p.op("vector", cast_fn, waits=cw, inc=swc, pub_drain=False)
                wland_free[0] = p.val(swc)
            cast_done[name] = p.val(swc)

        LN4 = [vtm[:, 10240 + i * 1024: 10240 + (i + 1) * 1024] for i in range(4)]
        K2 = [vtm[:, 14336:15360], vtm[:, 15360:16384]]
        nrm_allQ = vtm[:, 0:512].bitcast(F32)
        nrm_allK = vtm[:, 512:1024].bitcast(F32)
        scq = [vtm[:, 1024:2048], vtm[:, 2048:3072]]
        qdump = [vtm[:, 3072:5120].bitcast(F32),
                 vtm[:, 5120:7168].bitcast(F32)]
        kdump = [vtm[:, 7168:9216].bitcast(F32), P3[:]]

        def proj_matmuls(t, dest_ps, wsrc):
            def fn(e, t=t):
                ins = None
                for n in range(2):
                    for c in range(NCH):
                        ins = e.matmul(out=dest_ps[:, n * 512:(n + 1) * 512],
                                       lhsT=featA[:, c * S_LEN + t * 128: c * S_LEN + (t + 1) * 128],
                                       rhs=wsrc[:, c * D + n * 512: c * D + n * 512 + 512],
                                       start=(c == 0), stop=(c == NCH - 1))
                return ins
            return fn

        def rope_ops(e, t, src, dst):
            ev = _r3(src, 16, 0, 32)
            ov = _r3(src, 16, 32, 64)
            rev = _r3(dst, 16, 0, 32)
            rov = _r3(dst, 16, 32, 64)
            c3 = cos_c[:, t * 32:(t + 1) * 32].rearrange("p (u j) -> p u j", u=1).broadcast_to([128, 16, 32])
            s3 = sin_c[:, t * 32:(t + 1) * 32].rearrange("p (u j) -> p u j", u=1).broadcast_to([128, 16, 32])
            t3 = tmpA[:].rearrange("p (c j) -> p c j", c=16)
            e.tensor_mul(out=rev, in0=ev, in1=c3)
            e.tensor_mul(out=t3, in0=ov, in1=s3)
            e.tensor_sub(out=rev, in0=rev, in1=t3)
            e.tensor_mul(out=rov, in0=ov, in1=c3)
            e.tensor_mul(out=t3, in0=ev, in1=s3)
            return e.tensor_add(out=rov, in0=rov, in1=t3)

        for tn in range(4):
            p.op("sync", lambda e, tn=tn: dma(e, xt[tn][:],
                                              x_h.rearrange("(t p) d -> t p d", p=128)[tn]),
                 inc=s_x, amount=16)
        xdma_mark = [16 * (tn + 1) for tn in range(4)]

        load_weight("wq", wq_h, wbfA)
        load_weight("wk", wk_h, wbfB)

        apply_A = []
        lnevac = []
        qevac = []
        kevac = []
        qrope_v = []
        krope_v = []
        qred_marks = []
        kred_marks = []
        qsq_marks = []
        ksq_marks = []
        ktrq_pend = []
        evq_marks = []
        evk_marks = []

        def emit_reds(tr):
            p.op("vector", lambda e, tr=tr: e.tensor_reduce(
                out=nrm_allQ[:, tr * 16:(tr + 1) * 16].rearrange("p (h u) -> p h u", u=1),
                in_=_r3(qdump[tr % 2], 16), axis=AX.X, op=ALU.add),
                waits=[(sa, qsq_marks[tr])], inc=sv, pub_drain=False)
            qred_marks.append(p.val(sv))
            p.op("vector", lambda e, tr=tr: e.tensor_reduce(
                out=nrm_allK[:, tr * 16:(tr + 1) * 16].rearrange("p (h u) -> p h u", u=1),
                in_=_r3(kdump[tr % 2], 16), axis=AX.X, op=ALU.add),
                waits=[(sa, ksq_marks[tr])], inc=sv, pub_drain=False)
            kred_marks.append(p.val(sv))
        qproj_mms = []
        tailA = []
        for t in range(NT):
            g, j = divmod(t, 4)
            g4 = g * 4
            p.op("vector", lambda e, t=t: e.reduce_sum(
                out=stats["ssm"][:, t:t + 1], in_=xt[t % 4][:], axis=AX.X),
                waits=[(s_x, xdma_mark[t])] + ([(v_set, 1)] if t == 0 else []))
            p.op("scalar", lambda e, t=t: e.activation(
                out=P_sb[0][:], in_=xt[t % 4][:], func=AF.Square,
                accum_out=stats["ssq"][:, t:t + 1]),
                waits=[(s_x, xdma_mark[t])], inc=(sa if j == 3 else None))
            if j == 3:
                acc_A = p.val(sa)
            if t >= 4:
                tailA.pop(0)()
            if j == 3:
                m_sl = stats["m"][:, g4:g4 + 4]
                q_sl = stats["rstd"][:, g4:g4 + 4]
                msq_sl = stats["am"][:, g4:g4 + 4]
                var_sl = stats["nmr"][:, g4:g4 + 4]
                def gchain1(e, g4=g4, m_sl=m_sl, q_sl=q_sl, msq_sl=msq_sl, var_sl=var_sl):
                    e.drain()
                    e.tensor_scalar_mul(out=m_sl, in0=stats["ssm"][:, g4:g4 + 4],
                                        scalar1=1.0 / D)
                    e.tensor_scalar(out=q_sl, in0=stats["ssq"][:, g4:g4 + 4],
                                    scalar1=1.0 / D, scalar2=1e-5,
                                    op0=ALU.mult, op1=ALU.add)
                    e.drain()
                    e.tensor_mul(out=msq_sl, in0=m_sl, in1=m_sl)
                    e.drain()
                    return e.tensor_sub(out=var_sl, in0=q_sl, in1=msq_sl)
                p.op("vector", gchain1, waits=[(sa, acc_A)], inc=sv)
                gc1 = p.val(sv)
                p.op("scalar", lambda e, var_sl=var_sl, q_sl=q_sl: e.activation(
                    out=q_sl, in_=var_sl, func=AF.Sqrt),
                    waits=[(sv, gc1)], inc=sa)
                gsq = p.val(sa)
                def gchain2(e, m_sl=m_sl, q_sl=q_sl, var_sl=var_sl):
                    e.reciprocal(out=q_sl, in_=q_sl)
                    e.drain()
                    return e.scalar_tensor_tensor(out=var_sl, in0=m_sl, scalar=-1.0,
                                                  in1=q_sl, op0=ALU.mult, op1=ALU.mult)
                p.op("vector", gchain2, waits=[(sa, gsq)], inc=sv)
                gc2 = p.val(sv)
                def gapply(e, g4=g4):
                    ins = None
                    for jj in range(4):
                        ins = e.activation(out=LN4[jj], in_=xt[jj][:], func=AF.Identity,
                                           scale=stats["rstd"][:, g4 + jj:g4 + jj + 1],
                                           bias=stats["nmr"][:, g4 + jj:g4 + jj + 1])
                    return ins
                aw = [(sv, gc2)]
                if g >= 1:
                    aw.append((sv, lnevac[g4 - 1]))
                p.op("scalar", gapply, waits=aw, inc=sa, pub_drain=False)
                apply_A.append(p.val(sa))
                for tn in range(t + 1, min(t + 5, NT)):
                    p.op("gpsimd", lambda e, tn=tn: dma(
                        e, xt[tn % 4][:], x_h.rearrange("(t p) d -> t p d", p=128)[tn]),
                        waits=[(sa, apply_A[g])], inc=s_x, amount=16)
                    xdma_mark.append(p.val(s_x))

            def emit_tailA(t=t):
                def lntrx(e, t=t):
                    ins = None
                    for c in range(NCH):
                        ins = e.transpose(out=ptT[t % 2][:, c * 128:(c + 1) * 128],
                                          in_=LN4[t % 4][:, c * 128:(c + 1) * 128],
                                          identity=ident_bf[:])
                    return ins
                w = [(sa, apply_A[t // 4]), (g_set, 1)]
                if t >= 2:
                    w.append((sv, kevac[t - 2]))
                p.op("tensor", lntrx, waits=w, inc=stt, tdrain=True)
                m1 = p.val(stt)
                p.op("scalar", lambda e, t=t: e.copy(
                    out=_r3(featA[:], NCH, t * 128, (t + 1) * 128),
                    in_=_r3(ptT[t % 2][:], NCH)),
                    waits=[(stt, m1)], inc=sv, pub_drain=False)
                lnevac.append(p.val(sv))
                w = [(sv, lnevac[t]), (swc, cast_done["wq"])]
                if t >= 1:
                    w.append((sv, evq_marks[t - 1]))
                p.op("tensor", proj_matmuls(t, P0[:], wbfA), waits=w, inc=smm)
                qproj_mms.append(p.val(smm))
                if ktrq_pend:
                    ktrq_pend.pop(0)()
                w = [(sv, lnevac[t]), (swc, cast_done["wk"])]
                if t >= 1:
                    w.append((sv, evk_marks[t - 1]))
                p.op("tensor", proj_matmuls(t, P1[:], wbfB), waits=w, inc=smm)
                kmm = p.val(smm)
                p.op("vector", lambda e, t=t: e.tensor_add(
                    out=sc1[:], in0=P0[:], in1=bias0[:]),
                    waits=[(smm, qproj_mms[t]), (s_ld, setup_ld)],
                    inc=sv, pub_drain=False)
                evq_marks.append(p.val(sv))
                p.op("vector", lambda e, t=t: rope_ops(e, t, sc1[:], scq[t % 2]),
                     waits=[], inc=sv, pub_drain=False)
                qrope_v.append(p.val(sv))
                sqw = [(sv, qrope_v[t])]
                if t >= 2:
                    sqw.append((sv, qred_marks[t - 2]))
                p.op("scalar", lambda e, t=t: e.activation(
                    out=qdump[t % 2], in_=scq[t % 2], func=AF.Square),
                    waits=sqw, inc=sa, pub_drain=False)
                qsq_marks.append(p.val(sa))
                def qtrq(e, t=t):
                    ins = None
                    for c in range(NCH):
                        ins = e.transpose(out=ptT[t % 2][:, c * 128:(c + 1) * 128],
                                          in_=scq[t % 2][:, c * 128:(c + 1) * 128],
                                          identity=ident_bf[:])
                    return ins
                p.op("tensor", qtrq, waits=[(sv, qrope_v[t]), (sv, lnevac[t])],
                     inc=stt, tdrain=True)
                m2 = p.val(stt)
                p.op("scalar", lambda e, t=t: e.copy(
                    out=_r3(featQ[:], NCH, t * 128, (t + 1) * 128),
                    in_=_r3(ptT[t % 2][:], NCH)),
                    waits=[(stt, m2)], inc=sv, pub_drain=False)
                qevac.append(p.val(sv))
                p.op("vector", lambda e, t=t: e.tensor_add(
                    out=sc1[:], in0=P1[:], in1=bias1[:]),
                    waits=[(smm, kmm), (s_ld, setup_ld)],
                    inc=sv, pub_drain=False)
                evk_marks.append(p.val(sv))
                kvw = []
                if t >= 2:
                    kvw.append((sv, kevac[t - 2]))
                p.op("vector", lambda e, t=t: rope_ops(e, t, sc1[:], K2[t % 2]),
                     waits=kvw, inc=sv, pub_drain=False)
                krope_v.append(p.val(sv))
                kqw = [(sv, krope_v[t])]
                if t >= 2:
                    kqw.append((sv, kred_marks[t - 2]))
                p.op("scalar", lambda e, t=t: e.activation(
                    out=kdump[t % 2], in_=K2[t % 2], func=AF.Square),
                    waits=kqw, inc=sa, pub_drain=False)
                ksq_marks.append(p.val(sa))
                if t >= 1:
                    emit_reds(t - 1)
                def emit_ktrq(t=t):
                    def ktrq(e, t=t):
                        ins = None
                        for c in range(NCH):
                            ins = e.transpose(out=ptT[t % 2][:, c * 128:(c + 1) * 128],
                                              in_=K2[t % 2][:, c * 128:(c + 1) * 128],
                                              identity=ident_bf[:])
                        return ins
                    p.op("tensor", ktrq, waits=[(sv, krope_v[t]), (sv, qevac[t])],
                         inc=stt, tdrain=True)
                    m3 = p.val(stt)
                    p.op("scalar", lambda e, t=t: e.copy(
                        out=_r3(featK[:], NCH, t * 128, (t + 1) * 128),
                        in_=_r3(ptT[t % 2][:], NCH)),
                        waits=[(stt, m3)], inc=sv, pub_drain=False)
                    kevac.append(p.val(sv))
                ktrq_pend.append(emit_ktrq)

            tailA.append(emit_tailA)
        while tailA:
            tailA.pop(0)()
        while ktrq_pend:
            ktrq_pend.pop(0)()
        emit_reds(NT - 1)
        ln1_done_sv = lnevac[-1]
        q_done_sv = qevac[-1]
        k_done_sv = kevac[-1]

        def dump_feat(name, src, gate_sem, gate_val, width=None):
            if name not in dbg_h:
                return
            p.op("gpsimd", lambda e: e.dma_start(
                out=dbg_h[name], in_=src[:].bitcast(F32) if hasattr(src, "sbuf_range") or True else src),
                waits=[(gate_sem, gate_val)], inc=sdbg, amount=16)

        dump_feat("ln1f", featA, sv, ln1_done_sv)

        def sq_fin(e, src, dst):
            e.tensor_max(out=src[:, 0:128], in0=src[:, 0:128], in1=src[:, 128:256])
            e.drain()
            e.tensor_max(out=src[:, 0:64], in0=src[:, 0:64], in1=src[:, 64:128])
            e.drain()
            e.tensor_max(out=src[:, 0:32], in0=src[:, 0:32], in1=src[:, 32:64])
            e.drain()
            return e.tensor_max(out=dst[:], in0=src[:, 0:16], in1=src[:, 16:32])
        def qsq_fin(e):
            e.drain()
            sq_fin(e, nrm_allQ, qsqr)
            e.drain()
            return sq_fin(e, nrm_allK, ksqr)
        p.op("vector", qsq_fin, waits=[], inc=sv)
        sq_done_sv = p.val(sv)

        p.op("sync", lambda e: dma(e, bias0[:], bv_h[None, :].partition_broadcast(128)),
             waits=[(sv, qrope_v[NT - 1])], inc=s_ld, amount=16)
        bv_ld = p.val(s_ld)
        p.op("sync", lambda e: dma(e, bias1[:], bf_h[None, :].partition_broadcast(128)),
             waits=[(sv, krope_v[NT - 1])], inc=s_ld, amount=16)
        bf_ld = p.val(s_ld)
        load_weight("wv", wv_h, wbfA, first_waits=[(smm, qproj_mms[NT - 1])])

        v_marks = []
        for t in range(NT):
            waits = [(sv, kevac[t]), (swc, cast_done["wv"])]
            if t >= 2:
                waits.append((sv, v_marks[t - 2]))
            p.op("tensor", proj_matmuls(t, pq[t % 2], wbfA), waits=waits, inc=smm)
            smm_t = p.val(smm)
            def v_ev(e, t=t):
                base = t * 1040
                e.tensor_add(
                    out=vtm[:, base:base + 1040].rearrange("p (h c) -> p h c", c=65)[:, :, 0:64],
                    in0=_r3(pq[t % 2], 16), in1=_r3(bias0[:], 16))
                return e.memset(
                    vtm[:, base:base + 1040].rearrange("p (h c) -> p h c", c=65)[:, :, 64:65], 1.0)
            p.op("vector", v_ev, waits=[(smm, smm_t), (s_ld, bv_ld)], inc=sv, pub_drain=False)
            v_marks.append(p.val(sv))
        v_done_sv = p.val(sv)

        load_weight("wo", wo_h, wbfB)

        dump_feat("qf", featQ, sv, q_done_sv)
        if "qsqr" in dbg_h:
            p.op("gpsimd", lambda e: e.dma_start(out=dbg_h["qsqr"], in_=qsqr[:]),
                 waits=[(sv, sq_done_sv)], inc=sdbg, amount=16)
        if "ksqr" in dbg_h:
            p.op("gpsimd", lambda e: e.dma_start(out=dbg_h["ksqr"], in_=ksqr[:]),
                 waits=[(sv, sq_done_sv)], inc=sdbg, amount=16)
        dump_feat("wbfB", wbfB, sv, k_done_sv)
        dump_feat("wbfA2", wbfA, sv, v_done_sv)
        dump_feat("kf", featK, sv, k_done_sv)
        dump_feat("vtm", vtm, sv, v_done_sv)

        def cb_tp(e):
            e.transpose(out=P0[0:16, 0:128], in_=qsqr[:], identity=ident_f[:])
            return e.transpose(out=P1[0:16, 0:128], in_=ksqr[:], identity=ident_f[:])
        p.op("tensor", cb_tp, waits=[(sv, v_done_sv)], inc=smm, tdrain=True)
        cb_mm = p.val(smm)
        def cb_red(e):
            e.tensor_copy(out=cbtmp, in_=P0[0:16, 0:128])
            e.drain()
            e.reduce_max(out=cbt2, in_=cbtmp, axis=AX.X)
            e.drain()
            e.tensor_copy(out=cbtmp, in_=P1[0:16, 0:128])
            e.drain()
            e.reduce_max(out=cbt4[0:1, 0:1], in_=cbtmp[0:16, :].rearrange("p (u n) -> p u n", u=1),
                         axis=AX.X) if False else e.reduce_max(out=nrm_t[0:16, 0:1], in_=cbtmp, axis=AX.X)
            e.drain()
            return e.tensor_mul(out=cbt2, in0=cbt2, in1=nrm_t[0:16, 0:1])
        p.op("vector", cb_red, waits=[(smm, cb_mm)], inc=sv)
        cb_sv2 = p.val(sv)
        p.op("tensor", lambda e: e.transpose(out=P0[0:1, 0:16], in_=cbt2,
                                             identity=ident_f[0:16, 0:16]),
             waits=[(sv, cb_sv2)], inc=smm, tdrain=True)
        cb_mm2 = p.val(smm)
        def cb_fin(e):
            e.tensor_copy(out=cbt4, in_=P0[0:1, 0:16])
            e.drain()
            return e.reduce_max(out=cbt5, in_=cbt4, axis=AX.X)
        p.op("vector", cb_fin, waits=[(smm, cb_mm2)], inc=sv)
        cb_sv3 = p.val(sv)
        if "cbt2" in dbg_h:
            p.op("gpsimd", lambda e: e.dma_start(out=dbg_h["cbt2"], in_=cbt2),
                 waits=[(sv, cb_sv3)], inc=sdbg, amount=16)
            p.op("gpsimd", lambda e: e.dma_start(out=dbg_h["cbt5pre"], in_=cbt5),
                 waits=[(sv, cb_sv3)], inc=sdbg, amount=16)
        p.op("scalar", lambda e: e.activation(out=cbt5, in_=cbt5, func=AF.Sqrt),
             waits=[(sv, cb_sv3)] + ([(sdbg, p.val(sdbg))] if "cbt2" in dbg_h else []), inc=sa)
        cb_sa = p.val(sa)
        p.op("vector", lambda e: e.tensor_scalar_mul(out=cbt5, in0=cbt5, scalar1=-0.127),
             waits=[(sa, cb_sa)], inc=sv)
        cb_sv4 = p.val(sv)
        p.op("tensor", lambda e: e.matmul(out=P0[0:128, 0:1], lhsT=ones_row_f[:],
                                          rhs=cbt5, start=True, stop=True),
             waits=[(sv, cb_sv4)], inc=smm)
        cb_mm3 = p.val(smm)
        p.op("vector", lambda e: e.tensor_copy(out=negCb[:], in_=P0[0:128, 0:1]),
             waits=[(smm, cb_mm3)], inc=sv)
        negcb_sv = p.val(sv)
        if "negcb" in dbg_h:
            p.op("gpsimd", lambda e: e.dma_start(out=dbg_h["negcb"], in_=negCb[:]),
                 waits=[(sv, negcb_sv)], inc=sdbg, amount=16)

        iters = []
        for j in range(NPAIR):
            for qc in range(NQC):
                jp = (j * NQC + qc) % 2
                ktmax = 4 * qc + 4
                for kt in range(ktmax):
                    dj = kt - 4 * qc
                    qoff = max(0, 128 * dj)
                    iters.append((j, qc, kt, jp, qoff, 512 - qoff, qc * 512 + qoff,
                                  kt == 0, kt == ktmax - 1))
        NIT = len(iters)

        def st_fn_for(i):
            j, qc, kt, jp, qoff, n, qlo, first, last = iters[i]
            pp = i % 2
            def st_fn(e):
                e.matmul(out=pq[pp][:, qoff:qoff + n],
                         lhsT=featK[0:64, j * S_LEN + kt * 128: j * S_LEN + (kt + 1) * 128],
                         rhs=featQ[0:64, j * S_LEN + qlo: j * S_LEN + qlo + n],
                         start=True, stop=True, tile_position=(0, 0))
                return e.matmul(out=pq[pp][:, 512 + qoff:512 + qoff + n],
                                lhsT=featK[64:128, j * S_LEN + kt * 128: j * S_LEN + (kt + 1) * 128],
                                rhs=featQ[64:128, j * S_LEN + qlo: j * S_LEN + qlo + n],
                                start=True, stop=True, tile_position=(64, 0))
            return st_fn

        st_marks = {}
        exp_marks = {}
        pv_marks = {}
        oev_marks = []
        por_marks = []
        evac_pending = None

        def emit_st(i):
            stw = [(sv, negcb_sv)]
            if i >= 2:
                stw.append((sexp, exp_marks[i - 2]))
            p.op("tensor", st_fn_for(i), waits=stw, inc=smm)
            st_marks[i] = p.val(smm)

        def emit_exp_mask(i):
            j, qc, kt, jp, qoff, n, qlo, first, last = iters[i]
            pp = i % 2
            expw = [(smm, st_marks[i])]
            if i >= 2:
                expw.append((spv, pv_marks[i - 2]))
            p.op("scalar", lambda e, pp=pp: e.activation(
                out=P_sb[pp][:], in_=pq[pp], func=AF.Exp,
                bias=negCb[:], scale=0.125),
                waits=expw, inc=sexp, pub_drain=False)
            exp_marks[i] = p.val(sexp)
            mask_wait = []
            if kt - 4 * qc >= 0:
                def mk_fn(e, pp=pp, qoff=qoff):
                    v = _r3(P_sb[pp][:], 2, qoff, qoff + 128)
                    tv = _r3(tri2[:], 2)
                    return e.tensor_mul(out=v, in0=v, in1=tv)
                p.op("vector", mk_fn, waits=[(sexp, exp_marks[i])], inc=smsk,
                     pub_drain=False)
                mask_wait = [(smsk, p.val(smsk))]
            return mask_wait

        def emit_pv(i, mask_wait):
            j, qc, kt, jp, qoff, n, qlo, first, last = iters[i]
            pp = i % 2
            def pv_fn(e):
                e.matmul(out=po_lo[jp][0:128, qoff:qoff + n],
                         lhsT=vtm[:, kt * 1040 + (2 * j) * 65: kt * 1040 + (2 * j) * 65 + 128],
                         rhs=P_sb[pp][:, qoff:qoff + n],
                         start=first, stop=False)
                return e.matmul(out=po_hi[jp][0:128, qoff:qoff + n],
                                lhsT=vtm[:, kt * 1040 + (2 * j + 1) * 65: kt * 1040 + (2 * j + 1) * 65 + 128],
                                rhs=P_sb[pp][:, 512 + qoff:512 + qoff + n],
                                start=first, stop=False)
            pvw = [(sexp, exp_marks[i]), (sv, v_done_sv)] + mask_wait
            if first:
                if len(oev_marks) >= 2:
                    pvw.append((sv, oev_marks[-2]))
                if por_marks:
                    pvw.append((sv, por_marks[-1][0]))
            p.op("tensor", pv_fn, waits=pvw, inc=spv)
            pv_marks[i] = p.val(spv)

        def emit_evac(i):
            j, qc, kt, jp, qoff, n, qlo, first, last = iters[i]
            def oev_fn(e, j=j, qc=qc, jp=jp):
                e.tensor_copy(
                    out=featA[0:64, j * S_LEN + qc * 512: j * S_LEN + qc * 512 + 512],
                    in_=po_lo[jp][0:64, :])
                return e.tensor_copy(
                    out=featA[64:128, j * S_LEN + qc * 512: j * S_LEN + qc * 512 + 512],
                    in_=po_hi[jp][0:64, :])
            p.op("vector", oev_fn, waits=[(spv, pv_marks[i])], inc=sv, pub_drain=False)
            oev_marks.append(p.val(sv))
            rgw = [(spv, pv_marks[i])]
            if por_marks:
                rgw.append((s_rg, por_marks[-1][1]))
            def stage_fn(e, jp=jp):
                e.tensor_copy(out=sc1[0:1, 0:512], in_=po_lo[jp][64:65, :])
                return e.tensor_copy(out=sc1[0:1, 512:1024], in_=po_hi[jp][64:65, :])
            p.op("vector", stage_fn, waits=rgw, inc=sv)
            stage_sv = p.val(sv)
            p.op("gpsimd", lambda e, j=j, qc=qc: e.dma_start(
                out=rs_all[j:j + 1, qc * 1024:(qc + 1) * 1024], in_=sc1[0:1, :]),
                waits=[(sv, stage_sv)], inc=s_rg, amount=16)
            por_marks.append((stage_sv, p.val(s_rg)))

        pending_mask = {}
        emit_st(0)
        pending_mask[0] = emit_exp_mask(0)
        for i in range(NIT):
            if i + 1 < NIT:
                emit_st(i + 1)
                pending_mask[i + 1] = emit_exp_mask(i + 1)
            emit_pv(i, pending_mask.pop(i))
            if iters[i][8]:
                emit_evac(i)
        attn_done_sv = p.val(sv)

        p.op("vector", lambda e: e.reciprocal(out=rs_all, in_=rs_all),
             waits=[(sv, attn_done_sv), (s_rg, por_marks[-1][1])], inc=sv)
        rs_rc = p.val(sv)
        p.op("vector", lambda e: e.tensor_copy(out=rs_hi, in_=rs_all),
             waits=[(sv, rs_rc)], inc=sv)
        rs_hi_sv = p.val(sv)
        p.op("vector", lambda e: e.tensor_copy(out=rs_tmp, in_=rs_hi),
             waits=[(sv, rs_hi_sv)], inc=sv)
        rs_t_sv = p.val(sv)
        p.op("vector", lambda e: e.tensor_sub(out=rs_tmp, in0=rs_all, in1=rs_tmp),
             waits=[(sv, rs_t_sv)], inc=sv)
        rs_t2_sv = p.val(sv)
        p.op("vector", lambda e: e.tensor_copy(out=rs_lo, in_=rs_tmp),
             waits=[(sv, rs_t2_sv)], inc=sv)
        rs_sv = p.val(sv)
        p.op("gpsimd", lambda e: mask_init(e, mask8f), waits=[(sv, rs_sv)], inc=s_g)
        mask8f_g = p.val(s_g)
        if "rs" in dbg_h:
            p.op("gpsimd", lambda e: e.dma_start(out=dbg_h["rs"], in_=rs_all),
                 waits=[(sv, rs_sv)], inc=sdbg, amount=16)

        nrm_marks = []
        for j in range(NPAIR):
            for qc in range(NQC):
                jp2 = (j * NQC + qc) % 2
                def ex_fn(e, j=j, qc=qc, jp2=jp2):
                    dst = [P2, P3][jp2]
                    e.matmul(out=dst[0:64, 0:512], lhsT=mask8[:, j * 128: j * 128 + 64],
                             rhs=rs_hi[0:8, qc * 1024: qc * 1024 + 512],
                             start=True, stop=False, tile_position=(0, 0))
                    e.matmul(out=dst[0:64, 0:512], lhsT=mask8[:, j * 128: j * 128 + 64],
                             rhs=rs_lo[0:8, qc * 1024: qc * 1024 + 512],
                             start=False, stop=True, tile_position=(0, 0))
                    e.matmul(out=dst[64:128, 0:512], lhsT=mask8[:, j * 128 + 64: j * 128 + 128],
                             rhs=rs_hi[0:8, qc * 1024 + 512: qc * 1024 + 1024],
                             start=True, stop=False, tile_position=(0, 64))
                    return e.matmul(out=dst[64:128, 0:512], lhsT=mask8[:, j * 128 + 64: j * 128 + 128],
                                    rhs=rs_lo[0:8, qc * 1024 + 512: qc * 1024 + 1024],
                                    start=False, stop=True, tile_position=(0, 64))
                w = [(sv, rs_sv)]
                if len(nrm_marks) >= 2:
                    w.append((sv, nrm_marks[-2]))
                p.op("tensor", ex_fn, waits=w, inc=smm)
                ex_mm = p.val(smm)
                p.op("scalar", lambda e, jp2=jp2: e.copy(out=rb_sb[jp2],
                                                          in_=[P2, P3][jp2][:, 0:512]),
                     waits=[(smm, ex_mm)], inc=sa, pub_drain=False)
                ex_sa = p.val(sa)
                def nm_fn(e, j=j, qc=qc, jp2=jp2):
                    sl = featA[:, j * S_LEN + qc * 512: j * S_LEN + qc * 512 + 512]
                    return e.tensor_mul(out=sl, in0=sl, in1=rb_sb[jp2])
                p.op("vector", nm_fn, waits=[(sa, ex_sa)], inc=sv, pub_drain=False)
                nrm_marks.append(p.val(sv))
        onorm_sv = p.val(sv)
        dump_feat("of", featA, sv, onorm_sv)

        p.op("sync", lambda e: dma(e, bias0[:], bo_h[None, :].partition_broadcast(128)),
             waits=[(sv, v_done_sv)], inc=s_ld, amount=16)
        bo_ld = p.val(s_ld)

        wf_marks = []
        for c in range(NCH):
            p.op("sync", w_dma(wf_h, c), waits=[(swc, wland_free[0])], inc=s_ld, amount=16)
            ld_mark = p.val(s_ld)
            def q1(e, c=c):
                e.tensor_reduce(out=wsc[:, c:c + 1], in_=wland[:], axis=AX.X,
                                op=ALU.max, apply_absolute_value=True)
                e.drain()
                e.tensor_scalar_max(out=wsc[:, c:c + 1], in0=wsc[:, c:c + 1], scalar1=1e-8)
                e.drain()
                e.tensor_scalar_mul(out=wsc[:, c:c + 1], in0=wsc[:, c:c + 1], scalar1=1.0 / 127.0)
                e.drain()
                return e.reciprocal(out=wrxs[:, c:c + 1], in_=wsc[:, c:c + 1])
            p.op("vector", q1, waits=[(s_ld, ld_mark), (sv, attn_done_sv)], inc=sv)
            q1_sv = p.val(sv)
            def q2(e, c=c):
                e.activation(out=wfsc, in_=wland[:], func=AF.Identity,
                             bias=cmag[:], scale=wrxs[:, c:c + 1])
                return e.activation(out=wbfA[:, c * D:(c + 1) * D], in_=wfsc,
                                    func=AF.Identity, bias=cnmag[:], scale=cone[:])
            p.op("scalar", q2, waits=[(sv, q1_sv), (sv, onorm_sv)], inc=swc, pub_drain=False)
            wland_free[0] = p.val(swc)
            wf_marks.append(p.val(swc))
        wfq_done = p.val(swc)

        wt_marks = []
        for c in range(NCH):
            def wt_fn(e, c=c):
                ins = None
                for i in range(NCH):
                    ins = e.transpose(out=ptT[c % 2][:, i * 128:(i + 1) * 128],
                                      in_=wbfA[:, c * D + i * 128: c * D + (i + 1) * 128],
                                      identity=ident_bf[:])
                return ins
            w = [(swc, wfq_done), (sv, onorm_sv)]
            if c >= 2:
                w.append((sv, wt_marks[c - 2]))
            p.op("tensor", wt_fn, waits=w, inc=stt, tdrain=True)
            wt_mm = p.val(stt)
            def wt_ev(e, c=c):
                return e.tensor_copy(
                    out=wqTf.rearrange("p (i d) -> p i d", i=NCH)[:, :, c * 128:(c + 1) * 128],
                    in_=_r3(ptT[c % 2][:], NCH))
            p.op("vector", wt_ev, waits=[(stt, wt_mm)], inc=sv, pub_drain=False)
            wt_marks.append(p.val(sv))
        wqt_done_sv = p.val(sv)

        p.op("tensor", lambda e: e.transpose(out=P2[0:8, 512:640], in_=wsc[:], identity=ident_f[:]),
             waits=[(swc, wfq_done), (sv, wqt_done_sv)], inc=smm, tdrain=True)
        ws_mm = p.val(smm)
        p.op("vector", lambda e: e.tensor_copy(out=wsr[:], in_=P2[0:8, 512:640]),
             waits=[(smm, ws_mm)], inc=sv)
        ws_sv = p.val(sv)
        def wsb_fn(e):
            ins = None
            for c in range(NCH):
                ins = e.matmul(out=pf[:, c * 128:(c + 1) * 128],
                               lhsT=mask8f[:, c * 128:(c + 1) * 128], rhs=wsr[:],
                               start=True, stop=True)
            return ins
        p.op("tensor", wsb_fn, waits=[(sv, ws_sv), (g_set, 1), (s_g, mask8f_g)], inc=smm)
        wsb_mm = p.val(smm)
        p.op("vector", lambda e: e.tensor_copy(out=wsb, in_=pf), waits=[(smm, wsb_mm)], inc=sv,
             pub_drain=False)
        wsb_sv = p.val(sv)

        scq4 = [vtm[:, 12288:13312], vtm[:, 13312:14336],
                vtm[:, 14336:15360], vtm[:, 15360:16384]]
        x2_marks = []
        apply_marks = []
        obuf_marks = []
        xq_sc_marks = []
        gam_marks = []
        rd_marks = []
        trx_marks = []
        ffn_mms = []
        ep_marks = []
        acc_marks = []
        dbg_x2_ld = []
        tailT_queue = []
        tailV_queue = []
        for t in range(NT):
            g, j = divmod(t, 4)
            g4 = g * 4
            xw = []
            if t >= 2:
                xw = [(sv, x2_marks[t - 2])]
            p.op("gpsimd", lambda e, t=t: dma(e, xt[t % 4][:],
                                              x_h.rearrange("(t p) d -> t p d", p=128)[t]),
                 waits=xw, inc=s_x, amount=16)
            x_mark = p.val(s_x)
            def op_fn(e, t=t):
                ins = None
                for n2 in range(2):
                    for c in range(NCH):
                        ins = e.matmul(out=pq[t % 2][:, n2 * 512:(n2 + 1) * 512],
                                       lhsT=featA[:, c * S_LEN + t * 128: c * S_LEN + (t + 1) * 128],
                                       rhs=wbfB[:, c * D + n2 * 512: c * D + n2 * 512 + 512],
                                       start=(c == 0), stop=(c == NCH - 1))
                return ins
            w = [(sv, onorm_sv), (swc, cast_done["wo"]), (sv, wsb_sv)]
            if t >= 2:
                w.append((sv, x2_marks[t - 2]))
            p.op("tensor", op_fn, waits=w, inc=smm)
            op_mm = p.val(smm)
            if t >= 4:
                tailT_queue.pop(0)()
            def x2_fn(e, t=t):
                e.tensor_add(out=sc1[:], in0=pq[t % 2], in1=bias0[:])
                return e.tensor_add(out=x2g4[t % 4], in0=sc1[:], in1=xt[t % 4][:])
            x2w = [(smm, op_mm), (s_x, x_mark), (s_ld, bo_ld), (swc, wfq_done)]
            if t >= 4:
                x2w.append((sa, apply_marks[g - 1]))
                x2w.append((s_g, obuf_marks[t - 4]))
                if dbg_x2_ld:
                    x2w.append((sdbg, dbg_x2_ld[t - 4]))
            p.op("vector", x2_fn, waits=x2w, inc=sv, pub_drain=False)
            x2_marks.append(p.val(sv))
            x2_sv = p.val(sv)
            if "x2" in dbg_h:
                p.op("gpsimd", lambda e, t=t: e.dma_start(
                    out=dbg_h["x2"].rearrange("(t p) d -> t p d", p=128)[t], in_=x2g4[t % 4]),
                    waits=[(sv, x2_sv)], inc=sdbg, amount=16)
                dbg_x2_ld.append(p.val(sdbg))
            gw = [(sv, x2_sv), (s_ld, bf_ld)]
            if t >= 4:
                gw.append((sfin, 16 * (t - 3)))
            p.op("gpsimd", lambda e, t=t: e.tensor_add(
                out=obuf4[t % 4], in0=x2g4[t % 4], in1=bias1[:]), waits=gw, inc=s_g)
            obuf_marks.append(p.val(s_g))
            def accum_fn(e, t=t):
                e.activation(out=dum, in_=x2g4[t % 4], func=AF.Square,
                             accum_out=stats["ssq"][:, t:t + 1])
                return e.activation(out=dum, in_=x2g4[t % 4], func=AF.Identity,
                                    accum_out=stats["ssm"][:, t:t + 1])
            p.op("scalar", accum_fn, waits=[(sv, x2_sv)],
                 inc=(sa if j == 3 else None))
            if j == 3:
                acc_marks.append(p.val(sa))
                m_sl = stats["m"][:, g4:g4 + 4]
                q_sl = stats["rstd"][:, g4:g4 + 4]
                msq_sl = stats["am"][:, g4:g4 + 4]
                var_sl = stats["nmr"][:, g4:g4 + 4]
                def gchain1(e, g4=g4, m_sl=m_sl, q_sl=q_sl, msq_sl=msq_sl, var_sl=var_sl):
                    e.tensor_scalar_mul(out=m_sl, in0=stats["ssm"][:, g4:g4 + 4],
                                        scalar1=1.0 / D)
                    e.tensor_scalar(out=q_sl, in0=stats["ssq"][:, g4:g4 + 4],
                                    scalar1=1.0 / D, scalar2=1e-5,
                                    op0=ALU.mult, op1=ALU.add)
                    e.drain()
                    e.tensor_mul(out=msq_sl, in0=m_sl, in1=m_sl)
                    e.drain()
                    return e.tensor_sub(out=var_sl, in0=q_sl, in1=msq_sl)
                p.op("vector", gchain1, waits=[(sa, acc_marks[-1])], inc=sv)
                gc1 = p.val(sv)
                p.op("scalar", lambda e, g=g, var_sl=var_sl, q_sl=q_sl: e.activation(
                    out=q_sl, in_=var_sl, func=AF.Sqrt),
                    waits=[(sv, gc1)], inc=sa)
                gsq = p.val(sa)
                def gchain2(e, g=g, m_sl=m_sl, q_sl=q_sl, var_sl=var_sl):
                    e.reciprocal(out=q_sl, in_=q_sl)
                    e.drain()
                    return e.scalar_tensor_tensor(out=var_sl, in0=m_sl, scalar=-1.0,
                                                  in1=q_sl, op0=ALU.mult, op1=ALU.mult)
                p.op("vector", gchain2, waits=[(sa, gsq)], inc=sv)
                gc2 = p.val(sv)
                def gapply(e, g=g, g4=g4):
                    ins = None
                    for jj in range(4):
                        ins = e.activation(out=sc2g4[jj], in_=x2g4[jj], func=AF.Identity,
                                           scale=stats["rstd"][:, g4 + jj:g4 + jj + 1],
                                           bias=stats["nmr"][:, g4 + jj:g4 + jj + 1])
                    return ins
                aw = [(sv, gc2)]
                if gam_marks:
                    aw.append((sv, gam_marks[-1]))
                p.op("scalar", gapply, waits=aw, inc=sa, pub_drain=False)
                apply_marks.append(p.val(sa))
                def gam_fn(e, g=g, g4=g4):
                    e.tensor_reduce(
                        out=stats["am"][:, g4:g4 + 4].rearrange("p (t u) -> p t u", u=1),
                        in_=sc2g.rearrange("p (t d) -> p t d", t=4),
                        axis=AX.X, op=ALU.max, apply_absolute_value=True)
                    e.drain()
                    e.tensor_scalar(out=stats["xs"][:, g4:g4 + 4],
                                    in0=stats["am"][:, g4:g4 + 4],
                                    scalar1=1e-8, scalar2=1.0 / 127.0,
                                    op0=ALU.max, op1=ALU.mult)
                    e.drain()
                    return e.reciprocal(out=stats["rxs"][:, g4:g4 + 4],
                                        in_=stats["xs"][:, g4:g4 + 4])
                p.op("vector", gam_fn, waits=[(sa, apply_marks[-1])], inc=sv)
                gam_marks.append(p.val(sv))
                for jj in range(4):
                    tt_ = g4 + jj
                    def rd_fn(e, tt_=tt_, jj=jj):
                        e.activation(out=rdbuf, in_=sc2g4[jj], func=AF.Identity,
                                     scale=stats["rxs"][:, tt_:tt_ + 1], bias=cmag[:])
                        return e.activation(out=scq4[tt_ % 4], in_=rdbuf,
                                            func=AF.Identity, bias=cnmag[:])
                    rw = [(sv, gam_marks[-1])]
                    if tt_ >= 4:
                        rw.append((stt, trx_marks[tt_ - 4]))
                    p.op("scalar", rd_fn, waits=rw, inc=sa, pub_drain=False)
                    rd_marks.append(p.val(sa))

            def emit_tailT(t=t):
                def trx_fn(e, t=t):
                    ins = None
                    for c in range(NCH):
                        ins = e.transpose(out=ptT[t % 2][:, c * 128:(c + 1) * 128],
                                          in_=scq4[t % 4][:, c * 128:(c + 1) * 128],
                                          identity=ident_bf[:])
                    return ins
                w = [(sa, rd_marks[t]), (sv, wqt_done_sv)]
                if t >= 2:
                    w.append((sa, xq_sc_marks[t - 2]))
                p.op("tensor", trx_fn, waits=w, inc=stt, tdrain=True)
                trx_marks.append(p.val(stt))
                xw2 = [(stt, trx_marks[-1])]
                if ffn_mms:
                    xw2.append((smm, ffn_mms[-1]))
                p.op("scalar", lambda e, t=t: e.copy(
                    out=_r3(xqT, NCH), in_=_r3(ptT[t % 2][:], NCH)),
                    waits=xw2, inc=sa, pub_drain=False)
                xq_sc_marks.append(p.val(sa))
                def ffn_fn(e, t=t):
                    ins = None
                    for n2 in range(2):
                        for c in range(NCH):
                            ins = e.matmul(out=pf[:, n2 * 512:(n2 + 1) * 512],
                                           lhsT=xqT[:, c * 128:(c + 1) * 128],
                                           rhs=wqTf[:, c * D + n2 * 512: c * D + n2 * 512 + 512],
                                           start=(c == 0), stop=(c == NCH - 1))
                    return ins
                w = [(sa, xq_sc_marks[-1]), (sv, wsb_sv)]
                if ep_marks:
                    w.append((sv, ep_marks[-1]))
                p.op("tensor", ffn_fn, waits=w, inc=smm)
                ffn_mms.append(p.val(smm))

            def emit_tailV(t=t):
                def ep_fn(e, t=t):
                    e.scalar_tensor_tensor(out=t2buf, in0=pf,
                                           scalar=stats["xs"][:, t:t + 1],
                                           in1=wsb, op0=ALU.mult, op1=ALU.mult)
                    return e.tensor_add(out=obuf4[t % 4], in0=obuf4[t % 4], in1=t2buf)
                p.op("vector", ep_fn,
                     waits=[(smm, ffn_mms[t]), (s_g, obuf_marks[t])],
                     inc=sv, pub_drain=False)
                ep_marks.append(p.val(sv))
                p.op("sync", lambda e, t=t: e.dma_start(
                    out=out_h.rearrange("(t p) d -> t p d", p=128)[t], in_=obuf4[t % 4]),
                    waits=[(sv, ep_marks[-1])], inc=sfin, amount=16)

            if t >= 4:
                tailV_queue.pop(0)()
            tailT_queue.append(emit_tailT)
            tailV_queue.append(emit_tailV)

        while tailT_queue:
            tailT_queue.pop(0)()
            tailV_queue.pop(0)()

        p.op("gpsimd", lambda e: e.memset(ones_bf[:], 1.0),
             waits=[(sfin, 16 * NT), (s_ld, p.val(s_ld)), (sdbg, p.val(sdbg))])

        block = st.enter_context(nc.Block())
        p.emit(block)

    return nc


_PERM = None

def _perm():
    global _PERM
    if _PERM is None:
        idx = []
        for h in range(16):
            idx.extend((h * 64 + np.arange(0, 64, 2)).tolist())
            idx.extend((h * 64 + np.arange(1, 64, 2)).tolist())
        _PERM = np.asarray(idx)
    return _PERM


def _tables():
    inv = (1.0 / 10000.0 ** (np.arange(0, 64, 2, dtype=np.float32) / 64.0)).astype(np.float32)
    pos = np.arange(S_LEN, dtype=np.float32)
    th = (pos[:, None] * inv[None, :]).astype(np.float32)
    cos_t = np.cos(th.astype(np.float64)).astype(np.float32)
    sin_t = np.sin(th.astype(np.float64)).astype(np.float32)
    return cos_t, sin_t


def make_in_maps(x, wq_, bq_, wk_, bk_, wv_, bv_, wo_, bo_, wf_, bf_):
    perm = _perm()
    cos_t, sin_t = _tables()
    c = np.ascontiguousarray
    wq = c(np.asarray(wq_, np.float32)[:, perm])
    bq = c(np.asarray(bq_, np.float32)[perm])
    wk = c(np.asarray(wk_, np.float32)[:, perm])
    bk = c(np.asarray(bk_, np.float32)[perm])
    base = {"wq": wq, "bq": bq, "wk": wk, "bk": bk,
            "wv": c(np.asarray(wv_, np.float32)), "bv": c(np.asarray(bv_, np.float32)),
            "wo": c(np.asarray(wo_, np.float32)), "bo": c(np.asarray(bo_, np.float32)),
            "wf": c(np.asarray(wf_, np.float32)), "bfb": c(np.asarray(bf_, np.float32)),
            "cost": cos_t, "sint": sin_t}
    x = np.asarray(x, np.float32)
    return [dict(base, x=c(x[b])) for b in range(x.shape[0])]


def kernel(L0_attn_x, L0_attn_wq, L0_attn_bq, L0_attn_wk, L0_attn_bk,
           L0_attn_wv, L0_attn_bv, L0_attn_wo, L0_attn_bo,
           L0_ffn_weight, L0_ffn_bias):
    in_maps = make_in_maps(L0_attn_x, L0_attn_wq, L0_attn_bq, L0_attn_wk, L0_attn_bk,
                           L0_attn_wv, L0_attn_bv, L0_attn_wo, L0_attn_bo,
                           L0_ffn_weight, L0_ffn_bias)
    nc = build()
    res = run_bass_kernel_spmd(nc, in_maps, core_ids=list(range(8)))
    return np.stack([np.asarray(r["out"], np.float32) for r in res.results], axis=0)


# version 19
# speedup vs baseline: 1.0328x; 1.0328x over previous
import numpy as np
import concourse.bass as bass
import concourse.mybir as mybir
from concourse.bass_utils import run_bass_kernel_spmd
from contextlib import ExitStack

F32 = mybir.dt.float32
BF16 = mybir.dt.bfloat16
F16 = mybir.dt.float16
AF = mybir.ActivationFunctionType
ALU = mybir.AluOpType
AX = mybir.AxisListType

MAGIC = float(1.5 * 2.0 ** 23)
S_LEN = 2048
D = 1024
NT = 16
NCH = 8
NQC = 4
NPAIR = 8


class Prog:
    ENGS = ("sync", "scalar", "vector", "tensor", "gpsimd")

    def __init__(self, nc, stack, serialize=False):
        self.nc = nc
        self.stack = stack
        self.streams = {e: [] for e in self.ENGS}
        self.sems = {}
        self.counts = {}
        self.serialize = serialize

    def sem(self, name):
        self.sems[name] = self.stack.enter_context(self.nc.semaphore(name))
        self.counts[name] = 0
        return name

    def op(self, eng, fn, waits=(), inc=None, amount=1, tdrain=False, pub_drain=True):
        waits = list(waits)
        if self.serialize:
            skip = ("s_ld", "s_x", "sfin")
            waits = waits + [(s, self.counts[s]) for s in self.counts if s not in skip]
        if inc is not None:
            self.counts[inc] += amount
        self.streams[eng].append((waits, fn, inc, amount, tdrain, pub_drain))

    def val(self, s):
        return self.counts[s]

    def emit(self, block):
        for engname in self.ENGS:
            ops = self.streams[engname]
            if not ops:
                continue
            dec = getattr(block, engname)

            def _body(eng, _ops=ops, _drain=(engname in ("vector", "scalar"))):
                for waits, fn, inc, amount, tdrain, pub_drain in _ops:
                    for s, v in waits:
                        if v > 0:
                            eng.wait_ge(self.sems[s], v)
                    ins = fn(eng)
                    if inc is not None:
                        if (_drain and pub_drain) or tdrain:
                            eng.drain().then_inc(self.sems[inc], amount)
                        else:
                            ins.then_inc(self.sems[inc], amount)

            dec(_body)


def _r3(ap, c, lo=None, hi=None):
    v = ap.rearrange("p (c n) -> p c n", c=c)
    if lo is not None:
        v = v[:, :, lo:hi]
    return v


def build(debug=(), serialize=False):
    nc = bass.Bass("TRN2", target_bir_lowering=False)

    x_h = nc.dram_tensor("x", [S_LEN, D], F32, kind="ExternalInput").ap()
    wq_h = nc.dram_tensor("wq", [D, D], F32, kind="ExternalInput").ap()
    wk_h = nc.dram_tensor("wk", [D, D], F32, kind="ExternalInput").ap()
    wv_h = nc.dram_tensor("wv", [D, D], F32, kind="ExternalInput").ap()
    wo_h = nc.dram_tensor("wo", [D, D], F32, kind="ExternalInput").ap()
    wf_h = nc.dram_tensor("wf", [D, D], F32, kind="ExternalInput").ap()
    bq_h = nc.dram_tensor("bq", [D], F32, kind="ExternalInput").ap()
    bk_h = nc.dram_tensor("bk", [D], F32, kind="ExternalInput").ap()
    bv_h = nc.dram_tensor("bv", [D], F32, kind="ExternalInput").ap()
    bo_h = nc.dram_tensor("bo", [D], F32, kind="ExternalInput").ap()
    bf_h = nc.dram_tensor("bfb", [D], F32, kind="ExternalInput").ap()
    cos_h = nc.dram_tensor("cost", [S_LEN, 32], F32, kind="ExternalInput").ap()
    sin_h = nc.dram_tensor("sint", [S_LEN, 32], F32, kind="ExternalInput").ap()
    out_h = nc.dram_tensor("out", [S_LEN, D], F32, kind="ExternalOutput").ap()

    dbg_h = {}
    def dbg_tensor(name, shape):
        if name in debug:
            dbg_h[name] = nc.dram_tensor("d_" + name, shape, F32, kind="ExternalOutput").ap()

    dbg_tensor("ln1f", [128, NCH * S_LEN // 2])
    dbg_tensor("qf", [128, NCH * S_LEN // 2])
    dbg_tensor("kf", [128, NCH * S_LEN // 2])
    dbg_tensor("vtm", [128, (NT * 1040 + 64) // 2])
    dbg_tensor("wbfB", [128, NCH * D // 2])
    dbg_tensor("wbfA2", [128, NCH * D // 2])
    dbg_tensor("negcb", [128, 1])
    dbg_tensor("of", [128, NCH * S_LEN // 2])
    dbg_tensor("rs", [8, NQC * 1024])
    dbg_tensor("x2", [S_LEN, D])
    dbg_tensor("psb0", [128, 512])
    dbg_tensor("qsqr", [128, 16])
    dbg_tensor("cbt2", [16, 1])
    dbg_tensor("cbt5pre", [1, 1])
    dbg_tensor("ksqr", [128, 16])

    with ExitStack() as st:
        sb = lambda n, s, d: st.enter_context(nc.sbuf_tensor(n, s, d))
        p = Prog(nc, st, serialize=serialize)

        P0 = st.enter_context(nc.psum_tensor("P0", [128, 1024], F32))
        P1 = st.enter_context(nc.psum_tensor("P1", [128, 1024], F32))
        P2 = st.enter_context(nc.psum_tensor("P2", [128, 1024], F32))
        P3 = st.enter_context(nc.psum_tensor("P3", [128, 1024], F32))
        pq = [P0[:], P1[:]]
        ptT = [P2[:, 0:512].bitcast(BF16), P2[:, 512:1024].bitcast(BF16)]
        po_lo = [P2[:, 0:512], P3[:, 0:512]]
        po_hi = [P2[:, 512:1024], P3[:, 512:1024]]
        pf = P3[:]

        ident_bf = sb("ident_bf", [128, 128], BF16)
        ident_f = sb("ident_f", [128, 128], F32)
        tri2 = sb("tri2", [128, 256], BF16)
        ones_bf = sb("ones_bf", [128, 1], BF16)
        mask8 = sb("mask8", [8, 1024], BF16)
        ones_row_f = sb("ones_row_f", [1, 128], F32)
        cmag = sb("cmag", [128, 1], F32)
        cnmag = sb("cnmag", [128, 1], F32)
        cone = sb("cone", [128, 1], F32)
        cp2 = sb("cp2", [128, 8], F32)
        czero = cp2[:, 0:1]
        cm1 = cp2[:, 1:2]
        cinvd = cp2[:, 2:3]
        cninvd = cp2[:, 3:4]
        cepsd = cp2[:, 4:5]
        cnep8 = cp2[:, 5:6]
        ci127 = cp2[:, 6:7]
        cep8d = cp2[:, 7:8]
        cos_c = sb("cos_c", [128, 512], F32)
        sin_c = sb("sin_c", [128, 512], F32)
        bias0 = sb("bias0", [128, D], F32)
        bias1 = sb("bias1", [128, D], F32)

        xt = [sb(f"xt{i}", [128, D], F32) for i in range(4)]
        cbtmp = xt[0][0:16, 0:128]
        cbt2 = xt[0][0:16, 128:129]
        cbt4 = xt[0][0:1, 136:152]
        cbt5 = xt[0][0:1, 160:161]
        sc1 = sb("sc1", [128, D], F32)
        tmpA = sb("tmpA", [128, 512], F32)

        stats = {k: sb("st_" + k, [128, NT], F32)
                 for k in ("ssq", "ssm", "m", "rstd", "nmr", "am", "xs", "rxs")}
        wsc = sb("wsc", [128, 8], F32)
        wrxs = sb("wrxs", [128, 8], F32)
        qsqr = sb("qsqr", [128, 16], F32)
        ksqr = sb("ksqr", [128, 16], F32)
        nrm_t = sb("nrm_t", [128, 16], F32)
        negCb = sb("negCb", [128, 1], F32)
        wsr = sb("wsr", [8, 128], F32)

        wland = sb("wland", [128, D], F32)
        wbfA = sb("wbfA", [128, NCH * D], BF16)
        wbfB = sb("wbfB", [128, NCH * D], BF16)
        P_sb = [sb(f"psb{i}", [128, 1024], BF16) for i in range(2)]
        P_sb.append(xt[0][:].bitcast(BF16)[:, 0:1024])
        rstage = None

        featA = sb("featA", [128, NCH * S_LEN], BF16)
        vtm = sb("vtm", [128, NT * 1040 + 64], BF16)
        wsb = vtm[:, 0:2048].bitcast(F32)
        sc2 = vtm[:, 2048:4096].bitcast(F32)
        featQ = sb("featQ", [128, NCH * S_LEN], BF16)
        featK = sb("featK", [128, NCH * S_LEN], BF16)

        rs_all = wbfA[0:8, 0:8192].bitcast(F32)
        rs_hi = featK[0:8, 8192:12288]
        rs_lo = featK[0:8, 12288:16384]
        rs_tmp = featQ[0:8, 0:8192].bitcast(F32)
        mask8f = featQ[:, 2048:4096].bitcast(F32)[0:8, :]

        rb_sb = [featQ[:, 5120:6144].bitcast(F32), featQ[:, 6144:7168].bitcast(F32)]
        wfsc = featQ[:, 7168:9216].bitcast(F32)
        wqTf = featK[:, 0:8192]
        x2g = featQ[:, 0:8192].bitcast(F32)
        x2g4 = [x2g[:, i * 1024:(i + 1) * 1024] for i in range(4)]
        rdbuf = featQ[:, 8192:10240].bitcast(F32)
        t2buf = featQ[:, 10240:12288].bitcast(F32)
        xqT = featQ[:, 12288:13312]
        obuf = featK[:, 8192:16384].bitcast(F32)
        obuf4 = [obuf[:, i * 1024:(i + 1) * 1024] for i in range(4)]
        sc2g = vtm[:, 2048:10240].bitcast(F32)
        sc2g4 = [sc2g[:, i * 1024:(i + 1) * 1024] for i in range(4)]
        dum = vtm[:, 10240:12288].bitcast(F32)

        s_ld = p.sem("s_ld")
        s_x = p.sem("s_x")
        g_set = p.sem("g_set")
        v_set = p.sem("v_set")
        sa = p.sem("sa")
        sv = p.sem("sv")
        stt = p.sem("stt")
        smm = p.sem("smm")
        sexp = p.sem("sexp")
        smsk = p.sem("smsk")
        spv = p.sem("spv")
        swc = p.sem("swc")
        sfin = p.sem("sfin")
        sdbg = p.sem("sdbg")
        s_rg = p.sem("s_rg")
        s_g = p.sem("s_g")

        def dma(eng, out, in_):
            return eng.dma_start(out=out, in_=in_)

        p.op("sync", lambda e: dma(e, cos_c[:].rearrange("p (t j) -> p t j", t=NT),
                                   cos_h.rearrange("(t p) j -> p t j", p=128)), inc=s_ld, amount=16)
        p.op("sync", lambda e: dma(e, sin_c[:].rearrange("p (t j) -> p t j", t=NT),
                                   sin_h.rearrange("(t p) j -> p t j", p=128)), inc=s_ld, amount=16)
        p.op("sync", lambda e: dma(e, bias0[:], bq_h[None, :].partition_broadcast(128)), inc=s_ld, amount=16)
        p.op("sync", lambda e: dma(e, bias1[:], bk_h[None, :].partition_broadcast(128)), inc=s_ld, amount=16)
        setup_ld = p.val(s_ld)

        p.op("gpsimd", lambda e: e.memset(ident_bf[:], 1.0))
        p.op("gpsimd", lambda e: e.affine_select(ident_bf[:], ident_bf[:], pattern=[[-1, 128]],
                                                 compare_op=ALU.is_equal, fill=0.0, base=0,
                                                 channel_multiplier=1))
        p.op("gpsimd", lambda e: e.memset(ident_f[:], 1.0))
        p.op("gpsimd", lambda e: e.affine_select(ident_f[:], ident_f[:], pattern=[[-1, 128]],
                                                 compare_op=ALU.is_equal, fill=0.0, base=0,
                                                 channel_multiplier=1))
        p.op("gpsimd", lambda e: e.memset(tri2[:], 1.0))
        p.op("gpsimd", lambda e: e.affine_select(tri2[:], tri2[:], pattern=[[0, 2], [1, 128]],
                                                 compare_op=ALU.is_ge, fill=0.0, base=0,
                                                 channel_multiplier=-1))
        def mask_init(e, tgt):
            e.memset(tgt, 1.0)
            e.affine_select(tgt, tgt, pattern=[[1, 1024]], compare_op=ALU.is_ge,
                            fill=0.0, base=0, channel_multiplier=-128)
            return e.affine_select(tgt, tgt, pattern=[[-1, 1024]], compare_op=ALU.is_ge,
                                   fill=0.0, base=127, channel_multiplier=128)
        p.op("gpsimd", lambda e: mask_init(e, mask8[:]))
        p.op("gpsimd", lambda e: e.memset(ones_bf[:], 1.0), inc=g_set)

        p.op("vector", lambda e: e.memset(ones_row_f[:], 1.0))
        p.op("vector", lambda e: e.memset(cmag[:], MAGIC))
        p.op("vector", lambda e: e.memset(cnmag[:], -MAGIC))
        p.op("vector", lambda e: e.memset(czero, 0.0))
        p.op("vector", lambda e: e.memset(cm1, -1.0))
        p.op("vector", lambda e: e.memset(cinvd, 1.0 / D))
        p.op("vector", lambda e: e.memset(cninvd, -1.0 / D))
        p.op("vector", lambda e: e.memset(cepsd, 1e-5))
        p.op("vector", lambda e: e.memset(cnep8, -1e-8))
        p.op("vector", lambda e: e.memset(ci127, 1.0 / 127.0))
        p.op("vector", lambda e: e.memset(cep8d, 1e-8 / 127.0))
        p.op("vector", lambda e: e.memset(cone[:], 1.0), inc=v_set)

        wland_free = [0]
        cast_done = {}

        def w_dma(w_ap, c):
            def fn(e, w_ap=w_ap, c=c):
                return dma(e, wland[:], w_ap.rearrange("(c p) d -> c p d", p=128)[c])
            return fn

        def load_weight(name, w_ap, dest, first_waits=()):
            for c in range(NCH):
                p.op("sync", w_dma(w_ap, c), waits=[(swc, wland_free[0])], inc=s_ld, amount=16)
                def cast_fn(e, c=c, dest=dest):
                    return e.tensor_copy(out=dest[:, c * D:(c + 1) * D], in_=wland[:])
                cw = [(s_ld, p.val(s_ld))] + (list(first_waits) if c == 0 else [])
                p.op("vector", cast_fn, waits=cw, inc=swc, pub_drain=False)
                wland_free[0] = p.val(swc)
            cast_done[name] = p.val(swc)

        LN4 = [vtm[:, 10240 + i * 1024: 10240 + (i + 1) * 1024] for i in range(4)]
        K2 = [vtm[:, 14336:15360], vtm[:, 15360:16384]]
        nrm_allQ = vtm[:, 0:512].bitcast(F32)
        nrm_allK = vtm[:, 512:1024].bitcast(F32)
        scq = [vtm[:, 1024:2048], vtm[:, 2048:3072]]
        qdump = [vtm[:, 3072:5120].bitcast(F32),
                 vtm[:, 5120:7168].bitcast(F32)]
        kdump = [vtm[:, 7168:9216].bitcast(F32), P3[:]]

        def proj_matmuls(t, dest_ps, wsrc):
            def fn(e, t=t):
                ins = None
                for n in range(2):
                    for c in range(NCH):
                        ins = e.matmul(out=dest_ps[:, n * 512:(n + 1) * 512],
                                       lhsT=featA[:, c * S_LEN + t * 128: c * S_LEN + (t + 1) * 128],
                                       rhs=wsrc[:, c * D + n * 512: c * D + n * 512 + 512],
                                       start=(c == 0), stop=(c == NCH - 1))
                return ins
            return fn

        def rope_ops(e, t, src, dst):
            ev = _r3(src, 16, 0, 32)
            ov = _r3(src, 16, 32, 64)
            rev = _r3(dst, 16, 0, 32)
            rov = _r3(dst, 16, 32, 64)
            c3 = cos_c[:, t * 32:(t + 1) * 32].rearrange("p (u j) -> p u j", u=1).broadcast_to([128, 16, 32])
            s3 = sin_c[:, t * 32:(t + 1) * 32].rearrange("p (u j) -> p u j", u=1).broadcast_to([128, 16, 32])
            t3 = tmpA[:].rearrange("p (c j) -> p c j", c=16)
            e.tensor_mul(out=rev, in0=ev, in1=c3)
            e.tensor_mul(out=t3, in0=ov, in1=s3)
            e.tensor_sub(out=rev, in0=rev, in1=t3)
            e.tensor_mul(out=rov, in0=ov, in1=c3)
            e.tensor_mul(out=t3, in0=ev, in1=s3)
            return e.tensor_add(out=rov, in0=rov, in1=t3)

        for tn in range(4):
            p.op("sync", lambda e, tn=tn: dma(e, xt[tn][:],
                                              x_h.rearrange("(t p) d -> t p d", p=128)[tn]),
                 inc=s_x, amount=16)
        xdma_mark = [16 * (tn + 1) for tn in range(4)]

        load_weight("wq", wq_h, wbfA)
        load_weight("wk", wk_h, wbfB)

        apply_A = []
        lnevac = []
        qevac = []
        kevac = []
        qrope_v = []
        krope_v = []
        qred_marks = []
        kred_marks = []
        qsq_marks = []
        ksq_marks = []
        ktrq_pend = []
        evq_marks = []
        evk_marks = []

        def emit_reds(tr):
            p.op("vector", lambda e, tr=tr: e.tensor_reduce(
                out=nrm_allQ[:, tr * 16:(tr + 1) * 16].rearrange("p (h u) -> p h u", u=1),
                in_=_r3(qdump[tr % 2], 16), axis=AX.X, op=ALU.add),
                waits=[(sa, qsq_marks[tr])], inc=sv, pub_drain=False)
            qred_marks.append(p.val(sv))
            p.op("vector", lambda e, tr=tr: e.tensor_reduce(
                out=nrm_allK[:, tr * 16:(tr + 1) * 16].rearrange("p (h u) -> p h u", u=1),
                in_=_r3(kdump[tr % 2], 16), axis=AX.X, op=ALU.add),
                waits=[(sa, ksq_marks[tr])], inc=sv, pub_drain=False)
            kred_marks.append(p.val(sv))
        qproj_mms = []
        tailA = []
        for t in range(NT):
            g, j = divmod(t, 4)
            g4 = g * 4
            p.op("vector", lambda e, t=t: e.reduce_sum(
                out=stats["ssm"][:, t:t + 1], in_=xt[t % 4][:], axis=AX.X),
                waits=[(s_x, xdma_mark[t])] + ([(v_set, 1)] if t == 0 else []))
            p.op("scalar", lambda e, t=t: e.activation(
                out=P_sb[0][:], in_=xt[t % 4][:], func=AF.Square,
                accum_out=stats["ssq"][:, t:t + 1]),
                waits=[(s_x, xdma_mark[t])], inc=(sa if j == 3 else None))
            if j == 3:
                acc_A = p.val(sa)
            if t >= 4:
                tailA.pop(0)()
            if j == 3:
                m_sl = stats["m"][:, g4:g4 + 4]
                q_sl = stats["rstd"][:, g4:g4 + 4]
                msq_sl = stats["am"][:, g4:g4 + 4]
                var_sl = stats["nmr"][:, g4:g4 + 4]
                def gchain1(e, g4=g4, m_sl=m_sl, q_sl=q_sl, msq_sl=msq_sl, var_sl=var_sl):
                    e.drain()
                    e.tensor_scalar_mul(out=m_sl, in0=stats["ssm"][:, g4:g4 + 4],
                                        scalar1=1.0 / D)
                    e.tensor_scalar(out=q_sl, in0=stats["ssq"][:, g4:g4 + 4],
                                    scalar1=1.0 / D, scalar2=1e-5,
                                    op0=ALU.mult, op1=ALU.add)
                    e.drain()
                    e.tensor_mul(out=msq_sl, in0=m_sl, in1=m_sl)
                    e.drain()
                    return e.tensor_sub(out=var_sl, in0=q_sl, in1=msq_sl)
                p.op("vector", gchain1, waits=[(sa, acc_A)], inc=sv)
                gc1 = p.val(sv)
                p.op("scalar", lambda e, var_sl=var_sl, q_sl=q_sl: e.activation(
                    out=q_sl, in_=var_sl, func=AF.Sqrt),
                    waits=[(sv, gc1)], inc=sa)
                gsq = p.val(sa)
                def gchain2(e, m_sl=m_sl, q_sl=q_sl, var_sl=var_sl):
                    e.reciprocal(out=q_sl, in_=q_sl)
                    e.drain()
                    return e.scalar_tensor_tensor(out=var_sl, in0=m_sl, scalar=-1.0,
                                                  in1=q_sl, op0=ALU.mult, op1=ALU.mult)
                p.op("vector", gchain2, waits=[(sa, gsq)], inc=sv)
                gc2 = p.val(sv)
                def gapply(e, g4=g4):
                    ins = None
                    for jj in range(4):
                        ins = e.activation(out=LN4[jj], in_=xt[jj][:], func=AF.Identity,
                                           scale=stats["rstd"][:, g4 + jj:g4 + jj + 1],
                                           bias=stats["nmr"][:, g4 + jj:g4 + jj + 1])
                    return ins
                aw = [(sv, gc2)]
                if g >= 1:
                    aw.append((sv, lnevac[g4 - 1]))
                p.op("scalar", gapply, waits=aw, inc=sa, pub_drain=False)
                apply_A.append(p.val(sa))
                for tn in range(t + 1, min(t + 5, NT)):
                    p.op("gpsimd", lambda e, tn=tn: dma(
                        e, xt[tn % 4][:], x_h.rearrange("(t p) d -> t p d", p=128)[tn]),
                        waits=[(sa, apply_A[g])], inc=s_x, amount=16)
                    xdma_mark.append(p.val(s_x))

            def emit_tailA(t=t):
                def lntrx(e, t=t):
                    ins = None
                    for c in range(NCH):
                        ins = e.transpose(out=ptT[t % 2][:, c * 128:(c + 1) * 128],
                                          in_=LN4[t % 4][:, c * 128:(c + 1) * 128],
                                          identity=ident_bf[:])
                    return ins
                w = [(sa, apply_A[t // 4]), (g_set, 1)]
                if t >= 2:
                    w.append((sv, kevac[t - 2]))
                p.op("tensor", lntrx, waits=w, inc=stt, tdrain=True)
                m1 = p.val(stt)
                p.op("scalar", lambda e, t=t: e.copy(
                    out=_r3(featA[:], NCH, t * 128, (t + 1) * 128),
                    in_=_r3(ptT[t % 2][:], NCH)),
                    waits=[(stt, m1)], inc=sv, pub_drain=False)
                lnevac.append(p.val(sv))
                w = [(sv, lnevac[t]), (swc, cast_done["wq"])]
                if t >= 1:
                    w.append((sv, evq_marks[t - 1]))
                p.op("tensor", proj_matmuls(t, P0[:], wbfA), waits=w, inc=smm)
                qproj_mms.append(p.val(smm))
                if ktrq_pend:
                    ktrq_pend.pop(0)()
                w = [(sv, lnevac[t]), (swc, cast_done["wk"])]
                if t >= 1:
                    w.append((sv, evk_marks[t - 1]))
                p.op("tensor", proj_matmuls(t, P1[:], wbfB), waits=w, inc=smm)
                kmm = p.val(smm)
                p.op("vector", lambda e, t=t: e.tensor_add(
                    out=sc1[:], in0=P0[:], in1=bias0[:]),
                    waits=[(smm, qproj_mms[t]), (s_ld, setup_ld)],
                    inc=sv, pub_drain=False)
                evq_marks.append(p.val(sv))
                p.op("vector", lambda e, t=t: rope_ops(e, t, sc1[:], scq[t % 2]),
                     waits=[], inc=sv, pub_drain=False)
                qrope_v.append(p.val(sv))
                sqw = [(sv, qrope_v[t])]
                if t >= 2:
                    sqw.append((sv, qred_marks[t - 2]))
                p.op("scalar", lambda e, t=t: e.activation(
                    out=qdump[t % 2], in_=scq[t % 2], func=AF.Square),
                    waits=sqw, inc=sa, pub_drain=False)
                qsq_marks.append(p.val(sa))
                def qtrq(e, t=t):
                    ins = None
                    for c in range(NCH):
                        ins = e.transpose(out=ptT[t % 2][:, c * 128:(c + 1) * 128],
                                          in_=scq[t % 2][:, c * 128:(c + 1) * 128],
                                          identity=ident_bf[:])
                    return ins
                p.op("tensor", qtrq, waits=[(sv, qrope_v[t]), (sv, lnevac[t])],
                     inc=stt, tdrain=True)
                m2 = p.val(stt)
                p.op("scalar", lambda e, t=t: e.copy(
                    out=_r3(featQ[:], NCH, t * 128, (t + 1) * 128),
                    in_=_r3(ptT[t % 2][:], NCH)),
                    waits=[(stt, m2)], inc=sv, pub_drain=False)
                qevac.append(p.val(sv))
                p.op("vector", lambda e, t=t: e.tensor_add(
                    out=sc1[:], in0=P1[:], in1=bias1[:]),
                    waits=[(smm, kmm), (s_ld, setup_ld)],
                    inc=sv, pub_drain=False)
                evk_marks.append(p.val(sv))
                kvw = []
                if t >= 2:
                    kvw.append((sv, kevac[t - 2]))
                p.op("vector", lambda e, t=t: rope_ops(e, t, sc1[:], K2[t % 2]),
                     waits=kvw, inc=sv, pub_drain=False)
                krope_v.append(p.val(sv))
                kqw = [(sv, krope_v[t])]
                if t >= 2:
                    kqw.append((sv, kred_marks[t - 2]))
                p.op("scalar", lambda e, t=t: e.activation(
                    out=kdump[t % 2], in_=K2[t % 2], func=AF.Square),
                    waits=kqw, inc=sa, pub_drain=False)
                ksq_marks.append(p.val(sa))
                if t >= 1:
                    emit_reds(t - 1)
                def emit_ktrq(t=t):
                    def ktrq(e, t=t):
                        ins = None
                        for c in range(NCH):
                            ins = e.transpose(out=ptT[t % 2][:, c * 128:(c + 1) * 128],
                                              in_=K2[t % 2][:, c * 128:(c + 1) * 128],
                                              identity=ident_bf[:])
                        return ins
                    p.op("tensor", ktrq, waits=[(sv, krope_v[t]), (sv, qevac[t])],
                         inc=stt, tdrain=True)
                    m3 = p.val(stt)
                    p.op("scalar", lambda e, t=t: e.copy(
                        out=_r3(featK[:], NCH, t * 128, (t + 1) * 128),
                        in_=_r3(ptT[t % 2][:], NCH)),
                        waits=[(stt, m3)], inc=sv, pub_drain=False)
                    kevac.append(p.val(sv))
                ktrq_pend.append(emit_ktrq)

            tailA.append(emit_tailA)
        while tailA:
            tailA.pop(0)()
        while ktrq_pend:
            ktrq_pend.pop(0)()
        emit_reds(NT - 1)
        ln1_done_sv = lnevac[-1]
        q_done_sv = qevac[-1]
        k_done_sv = kevac[-1]

        def dump_feat(name, src, gate_sem, gate_val, width=None):
            if name not in dbg_h:
                return
            p.op("gpsimd", lambda e: e.dma_start(
                out=dbg_h[name], in_=src[:].bitcast(F32) if hasattr(src, "sbuf_range") or True else src),
                waits=[(gate_sem, gate_val)], inc=sdbg, amount=16)

        dump_feat("ln1f", featA, sv, ln1_done_sv)

        def sq_fin(e, src, dst):
            e.tensor_max(out=src[:, 0:128], in0=src[:, 0:128], in1=src[:, 128:256])
            e.drain()
            e.tensor_max(out=src[:, 0:64], in0=src[:, 0:64], in1=src[:, 64:128])
            e.drain()
            e.tensor_max(out=src[:, 0:32], in0=src[:, 0:32], in1=src[:, 32:64])
            e.drain()
            return e.tensor_max(out=dst[:], in0=src[:, 0:16], in1=src[:, 16:32])
        def qsq_fin(e):
            e.drain()
            sq_fin(e, nrm_allQ, qsqr)
            e.drain()
            return sq_fin(e, nrm_allK, ksqr)
        p.op("vector", qsq_fin, waits=[], inc=sv)
        sq_done_sv = p.val(sv)

        p.op("sync", lambda e: dma(e, bias0[:], bv_h[None, :].partition_broadcast(128)),
             waits=[(sv, qrope_v[NT - 1])], inc=s_ld, amount=16)
        bv_ld = p.val(s_ld)
        p.op("sync", lambda e: dma(e, bias1[:], bf_h[None, :].partition_broadcast(128)),
             waits=[(sv, krope_v[NT - 1])], inc=s_ld, amount=16)
        bf_ld = p.val(s_ld)
        load_weight("wv", wv_h, wbfA, first_waits=[(smm, qproj_mms[NT - 1])])

        v_marks = []
        for t in range(NT):
            waits = [(sv, kevac[t]), (swc, cast_done["wv"])]
            if t >= 2:
                waits.append((sv, v_marks[t - 2]))
            p.op("tensor", proj_matmuls(t, pq[t % 2], wbfA), waits=waits, inc=smm)
            smm_t = p.val(smm)
            def v_ev(e, t=t):
                base = t * 1040
                e.tensor_add(
                    out=vtm[:, base:base + 1040].rearrange("p (h c) -> p h c", c=65)[:, :, 0:64],
                    in0=_r3(pq[t % 2], 16), in1=_r3(bias0[:], 16))
                return e.memset(
                    vtm[:, base:base + 1040].rearrange("p (h c) -> p h c", c=65)[:, :, 64:65], 1.0)
            p.op("vector", v_ev, waits=[(smm, smm_t), (s_ld, bv_ld)], inc=sv, pub_drain=False)
            v_marks.append(p.val(sv))
        v_done_sv = p.val(sv)

        load_weight("wo", wo_h, wbfB)

        dump_feat("qf", featQ, sv, q_done_sv)
        if "qsqr" in dbg_h:
            p.op("gpsimd", lambda e: e.dma_start(out=dbg_h["qsqr"], in_=qsqr[:]),
                 waits=[(sv, sq_done_sv)], inc=sdbg, amount=16)
        if "ksqr" in dbg_h:
            p.op("gpsimd", lambda e: e.dma_start(out=dbg_h["ksqr"], in_=ksqr[:]),
                 waits=[(sv, sq_done_sv)], inc=sdbg, amount=16)
        dump_feat("wbfB", wbfB, sv, k_done_sv)
        dump_feat("wbfA2", wbfA, sv, v_done_sv)
        dump_feat("kf", featK, sv, k_done_sv)
        dump_feat("vtm", vtm, sv, v_done_sv)

        def cb_tp(e):
            e.transpose(out=P0[0:16, 0:128], in_=qsqr[:], identity=ident_f[:])
            return e.transpose(out=P1[0:16, 0:128], in_=ksqr[:], identity=ident_f[:])
        p.op("tensor", cb_tp, waits=[(sv, v_done_sv)], inc=smm, tdrain=True)
        cb_mm = p.val(smm)
        def cb_red(e):
            e.tensor_copy(out=cbtmp, in_=P0[0:16, 0:128])
            e.drain()
            e.reduce_max(out=cbt2, in_=cbtmp, axis=AX.X)
            e.drain()
            e.tensor_copy(out=cbtmp, in_=P1[0:16, 0:128])
            e.drain()
            e.reduce_max(out=cbt4[0:1, 0:1], in_=cbtmp[0:16, :].rearrange("p (u n) -> p u n", u=1),
                         axis=AX.X) if False else e.reduce_max(out=nrm_t[0:16, 0:1], in_=cbtmp, axis=AX.X)
            e.drain()
            return e.tensor_mul(out=cbt2, in0=cbt2, in1=nrm_t[0:16, 0:1])
        p.op("vector", cb_red, waits=[(smm, cb_mm)], inc=sv)
        cb_sv2 = p.val(sv)
        p.op("tensor", lambda e: e.transpose(out=P0[0:1, 0:16], in_=cbt2,
                                             identity=ident_f[0:16, 0:16]),
             waits=[(sv, cb_sv2)], inc=smm, tdrain=True)
        cb_mm2 = p.val(smm)
        def cb_fin(e):
            e.tensor_copy(out=cbt4, in_=P0[0:1, 0:16])
            e.drain()
            return e.reduce_max(out=cbt5, in_=cbt4, axis=AX.X)
        p.op("vector", cb_fin, waits=[(smm, cb_mm2)], inc=sv)
        cb_sv3 = p.val(sv)
        if "cbt2" in dbg_h:
            p.op("gpsimd", lambda e: e.dma_start(out=dbg_h["cbt2"], in_=cbt2),
                 waits=[(sv, cb_sv3)], inc=sdbg, amount=16)
            p.op("gpsimd", lambda e: e.dma_start(out=dbg_h["cbt5pre"], in_=cbt5),
                 waits=[(sv, cb_sv3)], inc=sdbg, amount=16)
        p.op("scalar", lambda e: e.activation(out=cbt5, in_=cbt5, func=AF.Sqrt),
             waits=[(sv, cb_sv3)] + ([(sdbg, p.val(sdbg))] if "cbt2" in dbg_h else []), inc=sa)
        cb_sa = p.val(sa)
        p.op("vector", lambda e: e.tensor_scalar_mul(out=cbt5, in0=cbt5, scalar1=-0.127),
             waits=[(sa, cb_sa)], inc=sv)
        cb_sv4 = p.val(sv)
        p.op("tensor", lambda e: e.matmul(out=P0[0:128, 0:1], lhsT=ones_row_f[:],
                                          rhs=cbt5, start=True, stop=True),
             waits=[(sv, cb_sv4)], inc=smm)
        cb_mm3 = p.val(smm)
        p.op("vector", lambda e: e.tensor_copy(out=negCb[:], in_=P0[0:128, 0:1]),
             waits=[(smm, cb_mm3)], inc=sv)
        negcb_sv = p.val(sv)
        if "negcb" in dbg_h:
            p.op("gpsimd", lambda e: e.dma_start(out=dbg_h["negcb"], in_=negCb[:]),
                 waits=[(sv, negcb_sv)], inc=sdbg, amount=16)

        iters = []
        for j in range(NPAIR):
            for qc in range(NQC):
                jp = (j * NQC + qc) % 2
                ktmax = 4 * qc + 4
                for kt in range(ktmax):
                    dj = kt - 4 * qc
                    qoff = max(0, 128 * dj)
                    iters.append((j, qc, kt, jp, qoff, 512 - qoff, qc * 512 + qoff,
                                  kt == 0, kt == ktmax - 1))
        NIT = len(iters)

        def st_fn_for(i):
            j, qc, kt, jp, qoff, n, qlo, first, last = iters[i]
            pp = i % 2
            def st_fn(e):
                e.matmul(out=pq[pp][:, qoff:qoff + n],
                         lhsT=featK[0:64, j * S_LEN + kt * 128: j * S_LEN + (kt + 1) * 128],
                         rhs=featQ[0:64, j * S_LEN + qlo: j * S_LEN + qlo + n],
                         start=True, stop=True, tile_position=(0, 0))
                return e.matmul(out=pq[pp][:, 512 + qoff:512 + qoff + n],
                                lhsT=featK[64:128, j * S_LEN + kt * 128: j * S_LEN + (kt + 1) * 128],
                                rhs=featQ[64:128, j * S_LEN + qlo: j * S_LEN + qlo + n],
                                start=True, stop=True, tile_position=(64, 0))
            return st_fn

        st_marks = {}
        exp_marks = {}
        pv_marks = {}
        oev_marks = []
        por_marks = []
        evac_pending = None

        def emit_st(i):
            stw = [(sv, negcb_sv)]
            if i >= 2:
                stw.append((sexp, exp_marks[i - 2]))
            p.op("tensor", st_fn_for(i), waits=stw, inc=smm)
            st_marks[i] = p.val(smm)

        def emit_exp_mask(i):
            j, qc, kt, jp, qoff, n, qlo, first, last = iters[i]
            pp = i % 2
            sb3 = i % 3
            expw = [(smm, st_marks[i])]
            if i >= 3:
                expw.append((spv, pv_marks[i - 3]))
            p.op("scalar", lambda e, pp=pp, sb3=sb3: e.activation(
                out=P_sb[sb3][:], in_=pq[pp], func=AF.Exp,
                bias=negCb[:], scale=0.125),
                waits=expw, inc=sexp, pub_drain=False)
            exp_marks[i] = p.val(sexp)
            mask_wait = []
            if kt - 4 * qc >= 0:
                def mk_fn(e, sb3=sb3, qoff=qoff):
                    v = _r3(P_sb[sb3][:], 2, qoff, qoff + 128)
                    tv = _r3(tri2[:], 2)
                    return e.tensor_mul(out=v, in0=v, in1=tv)
                p.op("vector", mk_fn, waits=[(sexp, exp_marks[i])], inc=smsk,
                     pub_drain=False)
                mask_wait = [(smsk, p.val(smsk))]
            return mask_wait

        def emit_pv(i, mask_wait):
            j, qc, kt, jp, qoff, n, qlo, first, last = iters[i]
            sb3 = i % 3
            def pv_fn(e, sb3=sb3):
                e.matmul(out=po_lo[jp][0:128, qoff:qoff + n],
                         lhsT=vtm[:, kt * 1040 + (2 * j) * 65: kt * 1040 + (2 * j) * 65 + 128],
                         rhs=P_sb[sb3][:, qoff:qoff + n],
                         start=first, stop=False)
                return e.matmul(out=po_hi[jp][0:128, qoff:qoff + n],
                                lhsT=vtm[:, kt * 1040 + (2 * j + 1) * 65: kt * 1040 + (2 * j + 1) * 65 + 128],
                                rhs=P_sb[sb3][:, 512 + qoff:512 + qoff + n],
                                start=first, stop=False)
            pvw = [(sexp, exp_marks[i]), (sv, v_done_sv)] + mask_wait
            if first:
                if len(oev_marks) >= 2:
                    pvw.append((sv, oev_marks[-2]))
                if len(por_marks) >= 2:
                    pvw.append((sv, por_marks[-2][0]))
            p.op("tensor", pv_fn, waits=pvw, inc=spv)
            pv_marks[i] = p.val(spv)

        def emit_evac(i):
            j, qc, kt, jp, qoff, n, qlo, first, last = iters[i]
            def oev_fn(e, j=j, qc=qc, jp=jp):
                e.tensor_copy(
                    out=featA[0:64, j * S_LEN + qc * 512: j * S_LEN + qc * 512 + 512],
                    in_=po_lo[jp][0:64, :])
                return e.tensor_copy(
                    out=featA[64:128, j * S_LEN + qc * 512: j * S_LEN + qc * 512 + 512],
                    in_=po_hi[jp][0:64, :])
            p.op("vector", oev_fn, waits=[(spv, pv_marks[i])], inc=sv, pub_drain=False)
            oev_marks.append(p.val(sv))
            rgw = [(spv, pv_marks[i])]
            if len(por_marks) >= 2:
                rgw.append((s_rg, por_marks[-2][1]))
            srb = [sc1[0:1, :], xt[1][0:1, :]][len(por_marks) % 2]
            def stage_fn(e, jp=jp, srb=srb):
                e.tensor_copy(out=srb[:, 0:512], in_=po_lo[jp][64:65, :])
                return e.tensor_copy(out=srb[:, 512:1024], in_=po_hi[jp][64:65, :])
            p.op("vector", stage_fn, waits=rgw, inc=sv)
            stage_sv = p.val(sv)
            p.op("gpsimd", lambda e, j=j, qc=qc, srb=srb: e.dma_start(
                out=rs_all[j:j + 1, qc * 1024:(qc + 1) * 1024], in_=srb),
                waits=[(sv, stage_sv)], inc=s_rg, amount=16)
            por_marks.append((stage_sv, p.val(s_rg)))

        pending_mask = {}
        emit_st(0)
        pending_mask[0] = emit_exp_mask(0)
        for i in range(NIT):
            if i + 1 < NIT:
                emit_st(i + 1)
                pending_mask[i + 1] = emit_exp_mask(i + 1)
            emit_pv(i, pending_mask.pop(i))
            if iters[i][8]:
                emit_evac(i)
        attn_done_sv = p.val(sv)

        p.op("vector", lambda e: e.reciprocal(out=rs_all, in_=rs_all),
             waits=[(sv, attn_done_sv), (s_rg, por_marks[-1][1])], inc=sv)
        rs_rc = p.val(sv)
        p.op("vector", lambda e: e.tensor_copy(out=rs_hi, in_=rs_all),
             waits=[(sv, rs_rc)], inc=sv)
        rs_hi_sv = p.val(sv)
        p.op("vector", lambda e: e.tensor_copy(out=rs_tmp, in_=rs_hi),
             waits=[(sv, rs_hi_sv)], inc=sv)
        rs_t_sv = p.val(sv)
        p.op("vector", lambda e: e.tensor_sub(out=rs_tmp, in0=rs_all, in1=rs_tmp),
             waits=[(sv, rs_t_sv)], inc=sv)
        rs_t2_sv = p.val(sv)
        p.op("vector", lambda e: e.tensor_copy(out=rs_lo, in_=rs_tmp),
             waits=[(sv, rs_t2_sv)], inc=sv)
        rs_sv = p.val(sv)
        p.op("gpsimd", lambda e: mask_init(e, mask8f), waits=[(sv, rs_sv)], inc=s_g)
        mask8f_g = p.val(s_g)
        if "rs" in dbg_h:
            p.op("gpsimd", lambda e: e.dma_start(out=dbg_h["rs"], in_=rs_all),
                 waits=[(sv, rs_sv)], inc=sdbg, amount=16)

        nrm_marks = []
        for j in range(NPAIR):
            for qc in range(NQC):
                jp2 = (j * NQC + qc) % 2
                def ex_fn(e, j=j, qc=qc, jp2=jp2):
                    dst = [P2, P3][jp2]
                    e.matmul(out=dst[0:64, 0:512], lhsT=mask8[:, j * 128: j * 128 + 64],
                             rhs=rs_hi[0:8, qc * 1024: qc * 1024 + 512],
                             start=True, stop=False, tile_position=(0, 0))
                    e.matmul(out=dst[0:64, 0:512], lhsT=mask8[:, j * 128: j * 128 + 64],
                             rhs=rs_lo[0:8, qc * 1024: qc * 1024 + 512],
                             start=False, stop=True, tile_position=(0, 0))
                    e.matmul(out=dst[64:128, 0:512], lhsT=mask8[:, j * 128 + 64: j * 128 + 128],
                             rhs=rs_hi[0:8, qc * 1024 + 512: qc * 1024 + 1024],
                             start=True, stop=False, tile_position=(0, 64))
                    return e.matmul(out=dst[64:128, 0:512], lhsT=mask8[:, j * 128 + 64: j * 128 + 128],
                                    rhs=rs_lo[0:8, qc * 1024 + 512: qc * 1024 + 1024],
                                    start=False, stop=True, tile_position=(0, 64))
                w = [(sv, rs_sv), (s_g, mask8f_g)]
                if len(nrm_marks) >= 2:
                    w.append((sv, nrm_marks[-2]))
                p.op("tensor", ex_fn, waits=w, inc=smm)
                ex_mm = p.val(smm)
                p.op("scalar", lambda e, jp2=jp2: e.copy(out=rb_sb[jp2],
                                                          in_=[P2, P3][jp2][:, 0:512]),
                     waits=[(smm, ex_mm)], inc=sa, pub_drain=False)
                ex_sa = p.val(sa)
                def nm_fn(e, j=j, qc=qc, jp2=jp2):
                    sl = featA[:, j * S_LEN + qc * 512: j * S_LEN + qc * 512 + 512]
                    return e.tensor_mul(out=sl, in0=sl, in1=rb_sb[jp2])
                p.op("vector", nm_fn, waits=[(sa, ex_sa)], inc=sv, pub_drain=False)
                nrm_marks.append(p.val(sv))
        onorm_sv = p.val(sv)
        dump_feat("of", featA, sv, onorm_sv)

        p.op("sync", lambda e: dma(e, bias0[:], bo_h[None, :].partition_broadcast(128)),
             waits=[(sv, v_done_sv)], inc=s_ld, amount=16)
        bo_ld = p.val(s_ld)

        wf_marks = []
        for c in range(NCH):
            p.op("sync", w_dma(wf_h, c), waits=[(swc, wland_free[0])], inc=s_ld, amount=16)
            ld_mark = p.val(s_ld)
            def q1(e, c=c):
                e.tensor_reduce(out=wsc[:, c:c + 1], in_=wland[:], axis=AX.X,
                                op=ALU.max, apply_absolute_value=True)
                e.drain()
                e.tensor_scalar_max(out=wsc[:, c:c + 1], in0=wsc[:, c:c + 1], scalar1=1e-8)
                e.drain()
                e.tensor_scalar_mul(out=wsc[:, c:c + 1], in0=wsc[:, c:c + 1], scalar1=1.0 / 127.0)
                e.drain()
                return e.reciprocal(out=wrxs[:, c:c + 1], in_=wsc[:, c:c + 1])
            p.op("vector", q1, waits=[(s_ld, ld_mark), (sv, attn_done_sv)], inc=sv)
            q1_sv = p.val(sv)
            def q2(e, c=c):
                e.activation(out=wfsc, in_=wland[:], func=AF.Identity,
                             bias=cmag[:], scale=wrxs[:, c:c + 1])
                return e.activation(out=wbfA[:, c * D:(c + 1) * D], in_=wfsc,
                                    func=AF.Identity, bias=cnmag[:], scale=cone[:])
            p.op("scalar", q2, waits=[(sv, q1_sv), (sv, onorm_sv)], inc=swc, pub_drain=False)
            wland_free[0] = p.val(swc)
            wf_marks.append(p.val(swc))
        wfq_done = p.val(swc)

        wt_marks = []
        for c in range(NCH):
            def wt_fn(e, c=c):
                ins = None
                for i in range(NCH):
                    ins = e.transpose(out=ptT[c % 2][:, i * 128:(i + 1) * 128],
                                      in_=wbfA[:, c * D + i * 128: c * D + (i + 1) * 128],
                                      identity=ident_bf[:])
                return ins
            w = [(swc, wfq_done), (sv, onorm_sv)]
            if c >= 2:
                w.append((sv, wt_marks[c - 2]))
            p.op("tensor", wt_fn, waits=w, inc=stt, tdrain=True)
            wt_mm = p.val(stt)
            def wt_ev(e, c=c):
                return e.tensor_copy(
                    out=wqTf.rearrange("p (i d) -> p i d", i=NCH)[:, :, c * 128:(c + 1) * 128],
                    in_=_r3(ptT[c % 2][:], NCH))
            p.op("vector", wt_ev, waits=[(stt, wt_mm)], inc=sv, pub_drain=False)
            wt_marks.append(p.val(sv))
        wqt_done_sv = p.val(sv)

        p.op("tensor", lambda e: e.transpose(out=P2[0:8, 512:640], in_=wsc[:], identity=ident_f[:]),
             waits=[(swc, wfq_done), (sv, wqt_done_sv)], inc=smm, tdrain=True)
        ws_mm = p.val(smm)
        p.op("vector", lambda e: e.tensor_copy(out=wsr[:], in_=P2[0:8, 512:640]),
             waits=[(smm, ws_mm)], inc=sv)
        ws_sv = p.val(sv)
        def wsb_fn(e):
            ins = None
            for c in range(NCH):
                ins = e.matmul(out=pf[:, c * 128:(c + 1) * 128],
                               lhsT=mask8f[:, c * 128:(c + 1) * 128], rhs=wsr[:],
                               start=True, stop=True)
            return ins
        p.op("tensor", wsb_fn, waits=[(sv, ws_sv), (g_set, 1), (s_g, mask8f_g)], inc=smm)
        wsb_mm = p.val(smm)
        p.op("vector", lambda e: e.tensor_copy(out=wsb, in_=pf), waits=[(smm, wsb_mm)], inc=sv,
             pub_drain=False)
        wsb_sv = p.val(sv)

        scq4 = [vtm[:, 12288:13312], vtm[:, 13312:14336],
                vtm[:, 14336:15360], vtm[:, 15360:16384]]
        x2_marks = []
        apply_marks = []
        obuf_marks = []
        xq_sc_marks = []
        gam_marks = []
        rd_marks = []
        trx_marks = []
        ffn_mms = []
        ep_marks = []
        acc_marks = []
        dbg_x2_ld = []
        tailT_queue = []
        tailV_queue = []
        for t in range(NT):
            g, j = divmod(t, 4)
            g4 = g * 4
            xw = []
            if t >= 2:
                xw = [(sv, x2_marks[t - 2])]
            p.op("gpsimd", lambda e, t=t: dma(e, xt[t % 4][:],
                                              x_h.rearrange("(t p) d -> t p d", p=128)[t]),
                 waits=xw, inc=s_x, amount=16)
            x_mark = p.val(s_x)
            def op_fn(e, t=t):
                ins = None
                for n2 in range(2):
                    for c in range(NCH):
                        ins = e.matmul(out=pq[t % 2][:, n2 * 512:(n2 + 1) * 512],
                                       lhsT=featA[:, c * S_LEN + t * 128: c * S_LEN + (t + 1) * 128],
                                       rhs=wbfB[:, c * D + n2 * 512: c * D + n2 * 512 + 512],
                                       start=(c == 0), stop=(c == NCH - 1))
                return ins
            w = [(sv, onorm_sv), (swc, cast_done["wo"]), (sv, wsb_sv)]
            if t >= 2:
                w.append((sv, x2_marks[t - 2]))
            p.op("tensor", op_fn, waits=w, inc=smm)
            op_mm = p.val(smm)
            if t >= 4:
                tailT_queue.pop(0)()
            def x2_fn(e, t=t):
                e.tensor_add(out=sc1[:], in0=pq[t % 2], in1=bias0[:])
                return e.tensor_add(out=x2g4[t % 4], in0=sc1[:], in1=xt[t % 4][:])
            x2w = [(smm, op_mm), (s_x, x_mark), (s_ld, bo_ld), (swc, wfq_done)]
            if t >= 4:
                x2w.append((sa, apply_marks[g - 1]))
                x2w.append((s_g, obuf_marks[t - 4]))
                if dbg_x2_ld:
                    x2w.append((sdbg, dbg_x2_ld[t - 4]))
            p.op("vector", x2_fn, waits=x2w, inc=sv, pub_drain=False)
            x2_marks.append(p.val(sv))
            x2_sv = p.val(sv)
            if "x2" in dbg_h:
                p.op("gpsimd", lambda e, t=t: e.dma_start(
                    out=dbg_h["x2"].rearrange("(t p) d -> t p d", p=128)[t], in_=x2g4[t % 4]),
                    waits=[(sv, x2_sv)], inc=sdbg, amount=16)
                dbg_x2_ld.append(p.val(sdbg))
            gw = [(sv, x2_sv), (s_ld, bf_ld)]
            if t >= 4:
                gw.append((sfin, 16 * (t - 3)))
            p.op("gpsimd", lambda e, t=t: e.tensor_add(
                out=obuf4[t % 4], in0=x2g4[t % 4], in1=bias1[:]), waits=gw, inc=s_g)
            obuf_marks.append(p.val(s_g))
            def accum_fn(e, t=t):
                e.activation(out=dum, in_=x2g4[t % 4], func=AF.Square,
                             accum_out=stats["ssq"][:, t:t + 1])
                return e.activation(out=dum, in_=x2g4[t % 4], func=AF.Identity,
                                    accum_out=stats["ssm"][:, t:t + 1])
            p.op("scalar", accum_fn, waits=[(sv, x2_sv)],
                 inc=(sa if j == 3 else None))
            if j == 3:
                acc_marks.append(p.val(sa))
                m_sl = stats["m"][:, g4:g4 + 4]
                q_sl = stats["rstd"][:, g4:g4 + 4]
                msq_sl = stats["am"][:, g4:g4 + 4]
                var_sl = stats["nmr"][:, g4:g4 + 4]
                def gchain1(e, g4=g4, m_sl=m_sl, q_sl=q_sl, msq_sl=msq_sl, var_sl=var_sl):
                    e.tensor_scalar_mul(out=m_sl, in0=stats["ssm"][:, g4:g4 + 4],
                                        scalar1=1.0 / D)
                    e.tensor_scalar(out=q_sl, in0=stats["ssq"][:, g4:g4 + 4],
                                    scalar1=1.0 / D, scalar2=1e-5,
                                    op0=ALU.mult, op1=ALU.add)
                    e.drain()
                    e.tensor_mul(out=msq_sl, in0=m_sl, in1=m_sl)
                    e.drain()
                    return e.tensor_sub(out=var_sl, in0=q_sl, in1=msq_sl)
                p.op("vector", gchain1, waits=[(sa, acc_marks[-1])], inc=sv)
                gc1 = p.val(sv)
                p.op("scalar", lambda e, g=g, var_sl=var_sl, q_sl=q_sl: e.activation(
                    out=q_sl, in_=var_sl, func=AF.Sqrt),
                    waits=[(sv, gc1)], inc=sa)
                gsq = p.val(sa)
                def gchain2(e, g=g, m_sl=m_sl, q_sl=q_sl, var_sl=var_sl):
                    e.reciprocal(out=q_sl, in_=q_sl)
                    e.drain()
                    return e.scalar_tensor_tensor(out=var_sl, in0=m_sl, scalar=-1.0,
                                                  in1=q_sl, op0=ALU.mult, op1=ALU.mult)
                p.op("vector", gchain2, waits=[(sa, gsq)], inc=sv)
                gc2 = p.val(sv)
                def gapply(e, g=g, g4=g4):
                    ins = None
                    for jj in range(4):
                        ins = e.activation(out=sc2g4[jj], in_=x2g4[jj], func=AF.Identity,
                                           scale=stats["rstd"][:, g4 + jj:g4 + jj + 1],
                                           bias=stats["nmr"][:, g4 + jj:g4 + jj + 1])
                    return ins
                aw = [(sv, gc2)]
                if gam_marks:
                    aw.append((sv, gam_marks[-1]))
                p.op("scalar", gapply, waits=aw, inc=sa, pub_drain=False)
                apply_marks.append(p.val(sa))
                def gam_fn(e, g=g, g4=g4):
                    e.tensor_reduce(
                        out=stats["am"][:, g4:g4 + 4].rearrange("p (t u) -> p t u", u=1),
                        in_=sc2g.rearrange("p (t d) -> p t d", t=4),
                        axis=AX.X, op=ALU.max, apply_absolute_value=True)
                    e.drain()
                    e.tensor_scalar(out=stats["xs"][:, g4:g4 + 4],
                                    in0=stats["am"][:, g4:g4 + 4],
                                    scalar1=1e-8, scalar2=1.0 / 127.0,
                                    op0=ALU.max, op1=ALU.mult)
                    e.drain()
                    return e.reciprocal(out=stats["rxs"][:, g4:g4 + 4],
                                        in_=stats["xs"][:, g4:g4 + 4])
                p.op("vector", gam_fn, waits=[(sa, apply_marks[-1])], inc=sv)
                gam_marks.append(p.val(sv))
                for jj in range(4):
                    tt_ = g4 + jj
                    def rd_fn(e, tt_=tt_, jj=jj):
                        e.activation(out=rdbuf, in_=sc2g4[jj], func=AF.Identity,
                                     scale=stats["rxs"][:, tt_:tt_ + 1], bias=cmag[:])
                        return e.activation(out=scq4[tt_ % 4], in_=rdbuf,
                                            func=AF.Identity, bias=cnmag[:])
                    rw = [(sv, gam_marks[-1])]
                    if tt_ >= 4:
                        rw.append((stt, trx_marks[tt_ - 4]))
                    p.op("scalar", rd_fn, waits=rw, inc=sa, pub_drain=False)
                    rd_marks.append(p.val(sa))

            def emit_tailT(t=t):
                def trx_fn(e, t=t):
                    ins = None
                    for c in range(NCH):
                        ins = e.transpose(out=ptT[t % 2][:, c * 128:(c + 1) * 128],
                                          in_=scq4[t % 4][:, c * 128:(c + 1) * 128],
                                          identity=ident_bf[:])
                    return ins
                w = [(sa, rd_marks[t]), (sv, wqt_done_sv)]
                if t >= 2:
                    w.append((sa, xq_sc_marks[t - 2]))
                p.op("tensor", trx_fn, waits=w, inc=stt, tdrain=True)
                trx_marks.append(p.val(stt))
                xw2 = [(stt, trx_marks[-1])]
                if ffn_mms:
                    xw2.append((smm, ffn_mms[-1]))
                p.op("scalar", lambda e, t=t: e.copy(
                    out=_r3(xqT, NCH), in_=_r3(ptT[t % 2][:], NCH)),
                    waits=xw2, inc=sa, pub_drain=False)
                xq_sc_marks.append(p.val(sa))
                def ffn_fn(e, t=t):
                    ins = None
                    for n2 in range(2):
                        for c in range(NCH):
                            ins = e.matmul(out=pf[:, n2 * 512:(n2 + 1) * 512],
                                           lhsT=xqT[:, c * 128:(c + 1) * 128],
                                           rhs=wqTf[:, c * D + n2 * 512: c * D + n2 * 512 + 512],
                                           start=(c == 0), stop=(c == NCH - 1))
                    return ins
                w = [(sa, xq_sc_marks[-1]), (sv, wsb_sv)]
                if ep_marks:
                    w.append((sv, ep_marks[-1]))
                p.op("tensor", ffn_fn, waits=w, inc=smm)
                ffn_mms.append(p.val(smm))

            def emit_tailV(t=t):
                def ep_fn(e, t=t):
                    e.scalar_tensor_tensor(out=t2buf, in0=pf,
                                           scalar=stats["xs"][:, t:t + 1],
                                           in1=wsb, op0=ALU.mult, op1=ALU.mult)
                    return e.tensor_add(out=obuf4[t % 4], in0=obuf4[t % 4], in1=t2buf)
                p.op("vector", ep_fn,
                     waits=[(smm, ffn_mms[t]), (s_g, obuf_marks[t])],
                     inc=sv, pub_drain=False)
                ep_marks.append(p.val(sv))
                p.op("sync", lambda e, t=t: e.dma_start(
                    out=out_h.rearrange("(t p) d -> t p d", p=128)[t], in_=obuf4[t % 4]),
                    waits=[(sv, ep_marks[-1])], inc=sfin, amount=16)

            if t >= 4:
                tailV_queue.pop(0)()
            tailT_queue.append(emit_tailT)
            tailV_queue.append(emit_tailV)

        while tailT_queue:
            tailT_queue.pop(0)()
            tailV_queue.pop(0)()

        p.op("gpsimd", lambda e: e.memset(ones_bf[:], 1.0),
             waits=[(sfin, 16 * NT), (s_ld, p.val(s_ld)), (sdbg, p.val(sdbg))])

        block = st.enter_context(nc.Block())
        p.emit(block)

    return nc


_PERM = None

def _perm():
    global _PERM
    if _PERM is None:
        idx = []
        for h in range(16):
            idx.extend((h * 64 + np.arange(0, 64, 2)).tolist())
            idx.extend((h * 64 + np.arange(1, 64, 2)).tolist())
        _PERM = np.asarray(idx)
    return _PERM


def _tables():
    inv = (1.0 / 10000.0 ** (np.arange(0, 64, 2, dtype=np.float32) / 64.0)).astype(np.float32)
    pos = np.arange(S_LEN, dtype=np.float32)
    th = (pos[:, None] * inv[None, :]).astype(np.float32)
    cos_t = np.cos(th.astype(np.float64)).astype(np.float32)
    sin_t = np.sin(th.astype(np.float64)).astype(np.float32)
    return cos_t, sin_t


def make_in_maps(x, wq_, bq_, wk_, bk_, wv_, bv_, wo_, bo_, wf_, bf_):
    perm = _perm()
    cos_t, sin_t = _tables()
    c = np.ascontiguousarray
    wq = c(np.asarray(wq_, np.float32)[:, perm])
    bq = c(np.asarray(bq_, np.float32)[perm])
    wk = c(np.asarray(wk_, np.float32)[:, perm])
    bk = c(np.asarray(bk_, np.float32)[perm])
    base = {"wq": wq, "bq": bq, "wk": wk, "bk": bk,
            "wv": c(np.asarray(wv_, np.float32)), "bv": c(np.asarray(bv_, np.float32)),
            "wo": c(np.asarray(wo_, np.float32)), "bo": c(np.asarray(bo_, np.float32)),
            "wf": c(np.asarray(wf_, np.float32)), "bfb": c(np.asarray(bf_, np.float32)),
            "cost": cos_t, "sint": sin_t}
    x = np.asarray(x, np.float32)
    return [dict(base, x=c(x[b])) for b in range(x.shape[0])]


def kernel(L0_attn_x, L0_attn_wq, L0_attn_bq, L0_attn_wk, L0_attn_bk,
           L0_attn_wv, L0_attn_bv, L0_attn_wo, L0_attn_bo,
           L0_ffn_weight, L0_ffn_bias):
    in_maps = make_in_maps(L0_attn_x, L0_attn_wq, L0_attn_bq, L0_attn_wk, L0_attn_bk,
                           L0_attn_wv, L0_attn_bv, L0_attn_wo, L0_attn_bo,
                           L0_ffn_weight, L0_ffn_bias)
    nc = build()
    res = run_bass_kernel_spmd(nc, in_maps, core_ids=list(range(8)))
    return np.stack([np.asarray(r["out"], np.float32) for r in res.results], axis=0)


# version 20 (final)
# speedup vs baseline: 1.0540x; 1.0206x over previous
import numpy as np
import concourse.bass as bass
import concourse.mybir as mybir
from concourse.bass_utils import run_bass_kernel_spmd
from contextlib import ExitStack

F32 = mybir.dt.float32
BF16 = mybir.dt.bfloat16
F16 = mybir.dt.float16
AF = mybir.ActivationFunctionType
ALU = mybir.AluOpType
AX = mybir.AxisListType

MAGIC = float(1.5 * 2.0 ** 23)
S_LEN = 2048
D = 1024
NT = 16
NCH = 8
NQC = 4
NPAIR = 8


class Prog:
    ENGS = ("sync", "scalar", "vector", "tensor", "gpsimd")

    def __init__(self, nc, stack, serialize=False):
        self.nc = nc
        self.stack = stack
        self.streams = {e: [] for e in self.ENGS}
        self.sems = {}
        self.counts = {}
        self.serialize = serialize

    def sem(self, name):
        self.sems[name] = self.stack.enter_context(self.nc.semaphore(name))
        self.counts[name] = 0
        return name

    def op(self, eng, fn, waits=(), inc=None, amount=1, tdrain=False, pub_drain=True):
        waits = list(waits)
        if self.serialize:
            skip = ("s_ld", "s_x", "sfin")
            waits = waits + [(s, self.counts[s]) for s in self.counts if s not in skip]
        if inc is not None:
            self.counts[inc] += amount
        self.streams[eng].append((waits, fn, inc, amount, tdrain, pub_drain))

    def val(self, s):
        return self.counts[s]

    def emit(self, block):
        for engname in self.ENGS:
            ops = self.streams[engname]
            if not ops:
                continue
            dec = getattr(block, engname)

            def _body(eng, _ops=ops, _drain=(engname in ("vector", "scalar"))):
                for waits, fn, inc, amount, tdrain, pub_drain in _ops:
                    for s, v in waits:
                        if v > 0:
                            eng.wait_ge(self.sems[s], v)
                    ins = fn(eng)
                    if inc is not None:
                        if (_drain and pub_drain) or tdrain:
                            eng.drain().then_inc(self.sems[inc], amount)
                        else:
                            ins.then_inc(self.sems[inc], amount)

            dec(_body)


def _r3(ap, c, lo=None, hi=None):
    v = ap.rearrange("p (c n) -> p c n", c=c)
    if lo is not None:
        v = v[:, :, lo:hi]
    return v


def build(debug=(), serialize=False):
    nc = bass.Bass("TRN2", target_bir_lowering=False)

    x_h = nc.dram_tensor("x", [S_LEN, D], F32, kind="ExternalInput").ap()
    wq_h = nc.dram_tensor("wq", [D, D], F32, kind="ExternalInput").ap()
    wk_h = nc.dram_tensor("wk", [D, D], F32, kind="ExternalInput").ap()
    wv_h = nc.dram_tensor("wv", [D, D], F32, kind="ExternalInput").ap()
    wo_h = nc.dram_tensor("wo", [D, D], F32, kind="ExternalInput").ap()
    wf_h = nc.dram_tensor("wf", [D, D], F32, kind="ExternalInput").ap()
    bq_h = nc.dram_tensor("bq", [D], F32, kind="ExternalInput").ap()
    bk_h = nc.dram_tensor("bk", [D], F32, kind="ExternalInput").ap()
    bv_h = nc.dram_tensor("bv", [D], F32, kind="ExternalInput").ap()
    bo_h = nc.dram_tensor("bo", [D], F32, kind="ExternalInput").ap()
    bf_h = nc.dram_tensor("bfb", [D], F32, kind="ExternalInput").ap()
    cos_h = nc.dram_tensor("cost", [S_LEN, 32], F32, kind="ExternalInput").ap()
    sin_h = nc.dram_tensor("sint", [S_LEN, 32], F32, kind="ExternalInput").ap()
    out_h = nc.dram_tensor("out", [S_LEN, D], F32, kind="ExternalOutput").ap()

    dbg_h = {}
    def dbg_tensor(name, shape):
        if name in debug:
            dbg_h[name] = nc.dram_tensor("d_" + name, shape, F32, kind="ExternalOutput").ap()

    dbg_tensor("ln1f", [128, NCH * S_LEN // 2])
    dbg_tensor("qf", [128, NCH * S_LEN // 2])
    dbg_tensor("kf", [128, NCH * S_LEN // 2])
    dbg_tensor("vtm", [128, (NT * 1040 + 64) // 2])
    dbg_tensor("wbfB", [128, NCH * D // 2])
    dbg_tensor("wbfA2", [128, NCH * D // 2])
    dbg_tensor("negcb", [128, 1])
    dbg_tensor("of", [128, NCH * S_LEN // 2])
    dbg_tensor("rs", [8, NQC * 1024])
    dbg_tensor("x2", [S_LEN, D])
    dbg_tensor("psb0", [128, 512])
    dbg_tensor("qsqr", [128, 16])
    dbg_tensor("cbt2", [16, 1])
    dbg_tensor("cbt5pre", [1, 1])
    dbg_tensor("ksqr", [128, 16])

    with ExitStack() as st:
        sb = lambda n, s, d: st.enter_context(nc.sbuf_tensor(n, s, d))
        p = Prog(nc, st, serialize=serialize)

        P0 = st.enter_context(nc.psum_tensor("P0", [128, 1024], F32))
        P1 = st.enter_context(nc.psum_tensor("P1", [128, 1024], F32))
        P2 = st.enter_context(nc.psum_tensor("P2", [128, 1024], F32))
        P3 = st.enter_context(nc.psum_tensor("P3", [128, 1024], F32))
        pq = [P0[:], P1[:]]
        ptT = [P2[:, 0:512].bitcast(BF16), P2[:, 512:1024].bitcast(BF16)]
        po_lo = [P2[:, 0:512], P3[:, 0:512]]
        po_hi = [P2[:, 512:1024], P3[:, 512:1024]]
        pf = P3[:]

        ident_bf = sb("ident_bf", [128, 128], BF16)
        ident_f = sb("ident_f", [128, 128], F32)
        tri2 = sb("tri2", [128, 256], BF16)
        ones_bf = sb("ones_bf", [128, 1], BF16)
        mask8 = sb("mask8", [8, 1024], BF16)
        ones_row_f = sb("ones_row_f", [1, 128], F32)
        cmag = sb("cmag", [128, 1], F32)
        cnmag = sb("cnmag", [128, 1], F32)
        cone = sb("cone", [128, 1], F32)
        cp2 = sb("cp2", [128, 8], F32)
        czero = cp2[:, 0:1]
        cm1 = cp2[:, 1:2]
        cinvd = cp2[:, 2:3]
        cninvd = cp2[:, 3:4]
        cepsd = cp2[:, 4:5]
        cnep8 = cp2[:, 5:6]
        ci127 = cp2[:, 6:7]
        cep8d = cp2[:, 7:8]
        cos_c = sb("cos_c", [128, 512], F32)
        sin_c = sb("sin_c", [128, 512], F32)
        bias0 = sb("bias0", [128, D], F32)
        bias1 = sb("bias1", [128, D], F32)

        xt = [sb(f"xt{i}", [128, D], F32) for i in range(4)]
        cbtmp = xt[0][0:16, 0:128]
        cbt2 = xt[0][0:16, 128:129]
        cbt4 = xt[0][0:1, 136:152]
        cbt5 = xt[0][0:1, 160:161]
        sc1 = sb("sc1", [128, D], F32)
        tmpA = sb("tmpA", [128, 512], F32)

        stats = {k: sb("st_" + k, [128, NT], F32)
                 for k in ("ssq", "ssm", "m", "rstd", "nmr", "am", "xs", "rxs")}
        wsc = sb("wsc", [128, 8], F32)
        wrxs = sb("wrxs", [128, 8], F32)
        qsqr = sb("qsqr", [128, 16], F32)
        ksqr = sb("ksqr", [128, 16], F32)
        nrm_t = sb("nrm_t", [128, 16], F32)
        negCb = sb("negCb", [128, 1], F32)
        wsr = sb("wsr", [8, 128], F32)

        wland = sb("wland", [128, D], F32)
        wbfA = sb("wbfA", [128, NCH * D], BF16)
        wbfB = sb("wbfB", [128, NCH * D], BF16)
        P_sb = [sb(f"psb{i}", [128, 1024], BF16) for i in range(2)]
        P_sb.append(xt[0][:].bitcast(BF16)[:, 0:1024])
        rstage = None

        featA = sb("featA", [128, NCH * S_LEN], BF16)
        vtm = sb("vtm", [128, NT * 1040 + 64], BF16)
        wsb = vtm[:, 0:2048].bitcast(F32)
        sc2 = vtm[:, 2048:4096].bitcast(F32)
        featQ = sb("featQ", [128, NCH * S_LEN], BF16)
        featK = sb("featK", [128, NCH * S_LEN], BF16)

        rs_all = wbfA[0:8, 0:8192].bitcast(F32)
        rs_hi = featK[0:8, 8192:12288]
        rs_lo = featK[0:8, 12288:16384]
        rs_tmp = featQ[0:8, 0:8192].bitcast(F32)
        mask8f = featQ[:, 2048:4096].bitcast(F32)[0:8, :]

        rb_sb = [featQ[:, 5120:6144].bitcast(F32), featQ[:, 6144:7168].bitcast(F32)]
        wfsc = featQ[:, 7168:9216].bitcast(F32)
        wqTf = featK[:, 0:8192]
        x2g = featQ[:, 0:8192].bitcast(F32)
        x2g4 = [x2g[:, i * 1024:(i + 1) * 1024] for i in range(4)]
        rdbuf = featQ[:, 8192:10240].bitcast(F32)
        t2buf = featQ[:, 10240:12288].bitcast(F32)
        xqT = featQ[:, 12288:13312]
        obuf = featK[:, 8192:16384].bitcast(F32)
        obuf4 = [obuf[:, i * 1024:(i + 1) * 1024] for i in range(4)]
        sc2g = vtm[:, 2048:10240].bitcast(F32)
        sc2g4 = [sc2g[:, i * 1024:(i + 1) * 1024] for i in range(4)]
        dum = vtm[:, 10240:12288].bitcast(F32)

        s_ld = p.sem("s_ld")
        s_x = p.sem("s_x")
        g_set = p.sem("g_set")
        v_set = p.sem("v_set")
        sa = p.sem("sa")
        sv = p.sem("sv")
        stt = p.sem("stt")
        smm = p.sem("smm")
        sexp = p.sem("sexp")
        smsk = p.sem("smsk")
        spv = p.sem("spv")
        swc = p.sem("swc")
        sfin = p.sem("sfin")
        sdbg = p.sem("sdbg")
        s_rg = p.sem("s_rg")
        s_g = p.sem("s_g")

        def dma(eng, out, in_):
            return eng.dma_start(out=out, in_=in_)

        p.op("sync", lambda e: dma(e, cos_c[:].rearrange("p (t j) -> p t j", t=NT),
                                   cos_h.rearrange("(t p) j -> p t j", p=128)), inc=s_ld, amount=16)
        p.op("sync", lambda e: dma(e, sin_c[:].rearrange("p (t j) -> p t j", t=NT),
                                   sin_h.rearrange("(t p) j -> p t j", p=128)), inc=s_ld, amount=16)
        p.op("sync", lambda e: dma(e, bias0[:], bq_h[None, :].partition_broadcast(128)), inc=s_ld, amount=16)
        p.op("sync", lambda e: dma(e, bias1[:], bk_h[None, :].partition_broadcast(128)), inc=s_ld, amount=16)
        setup_ld = p.val(s_ld)

        p.op("gpsimd", lambda e: e.memset(ident_bf[:], 1.0))
        p.op("gpsimd", lambda e: e.affine_select(ident_bf[:], ident_bf[:], pattern=[[-1, 128]],
                                                 compare_op=ALU.is_equal, fill=0.0, base=0,
                                                 channel_multiplier=1))
        p.op("gpsimd", lambda e: e.memset(ident_f[:], 1.0))
        p.op("gpsimd", lambda e: e.affine_select(ident_f[:], ident_f[:], pattern=[[-1, 128]],
                                                 compare_op=ALU.is_equal, fill=0.0, base=0,
                                                 channel_multiplier=1))
        p.op("gpsimd", lambda e: e.memset(tri2[:], 1.0))
        p.op("gpsimd", lambda e: e.affine_select(tri2[:], tri2[:], pattern=[[0, 2], [1, 128]],
                                                 compare_op=ALU.is_ge, fill=0.0, base=0,
                                                 channel_multiplier=-1))
        def mask_init(e, tgt):
            e.memset(tgt, 1.0)
            e.affine_select(tgt, tgt, pattern=[[1, 1024]], compare_op=ALU.is_ge,
                            fill=0.0, base=0, channel_multiplier=-128)
            return e.affine_select(tgt, tgt, pattern=[[-1, 1024]], compare_op=ALU.is_ge,
                                   fill=0.0, base=127, channel_multiplier=128)
        p.op("gpsimd", lambda e: mask_init(e, mask8[:]))
        p.op("gpsimd", lambda e: e.memset(ones_bf[:], 1.0), inc=g_set)

        p.op("vector", lambda e: e.memset(ones_row_f[:], 1.0))
        p.op("vector", lambda e: e.memset(cmag[:], MAGIC))
        p.op("vector", lambda e: e.memset(cnmag[:], -MAGIC))
        p.op("vector", lambda e: e.memset(czero, 0.0))
        p.op("vector", lambda e: e.memset(cm1, -1.0))
        p.op("vector", lambda e: e.memset(cinvd, 1.0 / D))
        p.op("vector", lambda e: e.memset(cninvd, -1.0 / D))
        p.op("vector", lambda e: e.memset(cepsd, 1e-5))
        p.op("vector", lambda e: e.memset(cnep8, -1e-8))
        p.op("vector", lambda e: e.memset(ci127, 1.0 / 127.0))
        p.op("vector", lambda e: e.memset(cep8d, 1e-8 / 127.0))
        p.op("vector", lambda e: e.memset(cone[:], 1.0), inc=v_set)

        wland_free = [0]
        cast_done = {}

        def w_dma(w_ap, c):
            def fn(e, w_ap=w_ap, c=c):
                return dma(e, wland[:], w_ap.rearrange("(c p) d -> c p d", p=128)[c])
            return fn

        def load_weight(name, w_ap, dest, first_waits=()):
            for c in range(NCH):
                p.op("sync", w_dma(w_ap, c), waits=[(swc, wland_free[0])], inc=s_ld, amount=16)
                def cast_fn(e, c=c, dest=dest):
                    return e.tensor_copy(out=dest[:, c * D:(c + 1) * D], in_=wland[:])
                cw = [(s_ld, p.val(s_ld))] + (list(first_waits) if c == 0 else [])
                p.op("vector", cast_fn, waits=cw, inc=swc, pub_drain=False)
                wland_free[0] = p.val(swc)
            cast_done[name] = p.val(swc)

        LN4 = [vtm[:, 10240 + i * 1024: 10240 + (i + 1) * 1024] for i in range(4)]
        K2 = [vtm[:, 14336:15360], vtm[:, 15360:16384]]
        nrm_allQ = vtm[:, 0:512].bitcast(F32)
        nrm_allK = vtm[:, 512:1024].bitcast(F32)
        scq = [vtm[:, 1024:2048], vtm[:, 2048:3072]]
        qdump = [vtm[:, 3072:5120].bitcast(F32),
                 vtm[:, 5120:7168].bitcast(F32)]
        kdump = [vtm[:, 7168:9216].bitcast(F32), P3[:]]

        def proj_matmuls(t, dest_ps, wsrc):
            def fn(e, t=t):
                ins = None
                for n in range(2):
                    for c in range(NCH):
                        ins = e.matmul(out=dest_ps[:, n * 512:(n + 1) * 512],
                                       lhsT=featA[:, c * S_LEN + t * 128: c * S_LEN + (t + 1) * 128],
                                       rhs=wsrc[:, c * D + n * 512: c * D + n * 512 + 512],
                                       start=(c == 0), stop=(c == NCH - 1))
                return ins
            return fn

        def rope_ops(e, t, src, dst):
            ev = _r3(src, 16, 0, 32)
            ov = _r3(src, 16, 32, 64)
            rev = _r3(dst, 16, 0, 32)
            rov = _r3(dst, 16, 32, 64)
            c3 = cos_c[:, t * 32:(t + 1) * 32].rearrange("p (u j) -> p u j", u=1).broadcast_to([128, 16, 32])
            s3 = sin_c[:, t * 32:(t + 1) * 32].rearrange("p (u j) -> p u j", u=1).broadcast_to([128, 16, 32])
            t3 = tmpA[:].rearrange("p (c j) -> p c j", c=16)
            e.tensor_mul(out=rev, in0=ev, in1=c3)
            e.tensor_mul(out=t3, in0=ov, in1=s3)
            e.tensor_sub(out=rev, in0=rev, in1=t3)
            e.tensor_mul(out=rov, in0=ov, in1=c3)
            e.tensor_mul(out=t3, in0=ev, in1=s3)
            return e.tensor_add(out=rov, in0=rov, in1=t3)

        for tn in range(4):
            p.op("sync", lambda e, tn=tn: dma(e, xt[tn][:],
                                              x_h.rearrange("(t p) d -> t p d", p=128)[tn]),
                 inc=s_x, amount=16)
        xdma_mark = [16 * (tn + 1) for tn in range(4)]

        load_weight("wq", wq_h, wbfA)
        load_weight("wk", wk_h, wbfB)

        apply_A = []
        lnevac = []
        qevac = []
        kevac = []
        qrope_v = []
        krope_v = []
        qred_marks = []
        kred_marks = []
        qsq_marks = []
        ksq_marks = []
        ktrq_pend = []
        evq_marks = []
        evk_marks = []

        def emit_reds(tr):
            p.op("vector", lambda e, tr=tr: e.tensor_reduce(
                out=nrm_allQ[:, tr * 16:(tr + 1) * 16].rearrange("p (h u) -> p h u", u=1),
                in_=_r3(qdump[tr % 2], 16), axis=AX.X, op=ALU.add),
                waits=[(sa, qsq_marks[tr])], inc=sv, pub_drain=False)
            qred_marks.append(p.val(sv))
            p.op("vector", lambda e, tr=tr: e.tensor_reduce(
                out=nrm_allK[:, tr * 16:(tr + 1) * 16].rearrange("p (h u) -> p h u", u=1),
                in_=_r3(kdump[tr % 2], 16), axis=AX.X, op=ALU.add),
                waits=[(sa, ksq_marks[tr])], inc=sv, pub_drain=False)
            kred_marks.append(p.val(sv))
        qproj_mms = []
        tailA = []
        for t in range(NT):
            g, j = divmod(t, 4)
            g4 = g * 4
            def accA_fn(e, t=t):
                e.activation(out=P_sb[0][:], in_=xt[t % 4][:], func=AF.Square,
                             accum_out=stats["ssq"][:, t:t + 1])
                return e.activation(out=P_sb[0][:], in_=xt[t % 4][:], func=AF.Identity,
                                    accum_out=stats["ssm"][:, t:t + 1])
            p.op("scalar", accA_fn,
                 waits=[(s_x, xdma_mark[t])] + ([(v_set, 1)] if t == 0 else []),
                 inc=(sa if j == 3 else None))
            if j == 3:
                acc_A = p.val(sa)
            if t >= 4:
                tailA.pop(0)()
            if j == 3:
                m_sl = stats["m"][:, g4:g4 + 4]
                q_sl = stats["rstd"][:, g4:g4 + 4]
                msq_sl = stats["am"][:, g4:g4 + 4]
                var_sl = stats["nmr"][:, g4:g4 + 4]
                def gchain1(e, g4=g4, m_sl=m_sl, q_sl=q_sl, msq_sl=msq_sl, var_sl=var_sl):
                    e.drain()
                    e.tensor_scalar_mul(out=m_sl, in0=stats["ssm"][:, g4:g4 + 4],
                                        scalar1=1.0 / D)
                    e.tensor_scalar(out=q_sl, in0=stats["ssq"][:, g4:g4 + 4],
                                    scalar1=1.0 / D, scalar2=1e-5,
                                    op0=ALU.mult, op1=ALU.add)
                    e.drain()
                    e.tensor_mul(out=msq_sl, in0=m_sl, in1=m_sl)
                    e.drain()
                    return e.tensor_sub(out=var_sl, in0=q_sl, in1=msq_sl)
                p.op("vector", gchain1, waits=[(sa, acc_A)], inc=sv)
                gc1 = p.val(sv)
                p.op("scalar", lambda e, var_sl=var_sl, q_sl=q_sl: e.activation(
                    out=q_sl, in_=var_sl, func=AF.Sqrt),
                    waits=[(sv, gc1)], inc=sa)
                gsq = p.val(sa)
                def gchain2(e, m_sl=m_sl, q_sl=q_sl, var_sl=var_sl):
                    e.reciprocal(out=q_sl, in_=q_sl)
                    e.drain()
                    return e.scalar_tensor_tensor(out=var_sl, in0=m_sl, scalar=-1.0,
                                                  in1=q_sl, op0=ALU.mult, op1=ALU.mult)
                p.op("vector", gchain2, waits=[(sa, gsq)], inc=sv)
                gc2 = p.val(sv)
                def gapply(e, g4=g4):
                    ins = None
                    for jj in range(4):
                        ins = e.activation(out=LN4[jj], in_=xt[jj][:], func=AF.Identity,
                                           scale=stats["rstd"][:, g4 + jj:g4 + jj + 1],
                                           bias=stats["nmr"][:, g4 + jj:g4 + jj + 1])
                    return ins
                aw = [(sv, gc2)]
                if g >= 1:
                    aw.append((sv, lnevac[g4 - 1]))
                p.op("scalar", gapply, waits=aw, inc=sa, pub_drain=False)
                apply_A.append(p.val(sa))
                for tn in range(t + 1, min(t + 5, NT)):
                    p.op("gpsimd", lambda e, tn=tn: dma(
                        e, xt[tn % 4][:], x_h.rearrange("(t p) d -> t p d", p=128)[tn]),
                        waits=[(sa, apply_A[g])], inc=s_x, amount=16)
                    xdma_mark.append(p.val(s_x))

            def emit_tailA(t=t):
                def lntrx(e, t=t):
                    ins = None
                    for c in range(NCH):
                        ins = e.transpose(out=ptT[t % 2][:, c * 128:(c + 1) * 128],
                                          in_=LN4[t % 4][:, c * 128:(c + 1) * 128],
                                          identity=ident_bf[:])
                    return ins
                w = [(sa, apply_A[t // 4]), (g_set, 1)]
                if t >= 2:
                    w.append((sv, kevac[t - 2]))
                p.op("tensor", lntrx, waits=w, inc=stt, tdrain=True)
                m1 = p.val(stt)
                p.op("scalar", lambda e, t=t: e.copy(
                    out=_r3(featA[:], NCH, t * 128, (t + 1) * 128),
                    in_=_r3(ptT[t % 2][:], NCH)),
                    waits=[(stt, m1)], inc=sv, pub_drain=False)
                lnevac.append(p.val(sv))
                w = [(sv, lnevac[t]), (swc, cast_done["wq"])]
                if t >= 1:
                    w.append((sv, evq_marks[t - 1]))
                p.op("tensor", proj_matmuls(t, P0[:], wbfA), waits=w, inc=smm)
                qproj_mms.append(p.val(smm))
                if ktrq_pend:
                    ktrq_pend.pop(0)()
                w = [(sv, lnevac[t]), (swc, cast_done["wk"])]
                if t >= 1:
                    w.append((sv, evk_marks[t - 1]))
                p.op("tensor", proj_matmuls(t, P1[:], wbfB), waits=w, inc=smm)
                kmm = p.val(smm)
                p.op("vector", lambda e, t=t: e.tensor_add(
                    out=sc1[:], in0=P0[:], in1=bias0[:]),
                    waits=[(smm, qproj_mms[t]), (s_ld, setup_ld)],
                    inc=sv, pub_drain=False)
                evq_marks.append(p.val(sv))
                p.op("vector", lambda e, t=t: rope_ops(e, t, sc1[:], scq[t % 2]),
                     waits=[], inc=sv, pub_drain=False)
                qrope_v.append(p.val(sv))
                sqw = [(sv, qrope_v[t])]
                if t >= 2:
                    sqw.append((sv, qred_marks[t - 2]))
                p.op("scalar", lambda e, t=t: e.activation(
                    out=qdump[t % 2], in_=scq[t % 2], func=AF.Square),
                    waits=sqw, inc=sa, pub_drain=False)
                qsq_marks.append(p.val(sa))
                def qtrq(e, t=t):
                    ins = None
                    for c in range(NCH):
                        ins = e.transpose(out=ptT[t % 2][:, c * 128:(c + 1) * 128],
                                          in_=scq[t % 2][:, c * 128:(c + 1) * 128],
                                          identity=ident_bf[:])
                    return ins
                p.op("tensor", qtrq, waits=[(sv, qrope_v[t]), (sv, lnevac[t])],
                     inc=stt, tdrain=True)
                m2 = p.val(stt)
                p.op("scalar", lambda e, t=t: e.copy(
                    out=_r3(featQ[:], NCH, t * 128, (t + 1) * 128),
                    in_=_r3(ptT[t % 2][:], NCH)),
                    waits=[(stt, m2)], inc=sv, pub_drain=False)
                qevac.append(p.val(sv))
                p.op("vector", lambda e, t=t: e.tensor_add(
                    out=sc1[:], in0=P1[:], in1=bias1[:]),
                    waits=[(smm, kmm), (s_ld, setup_ld)],
                    inc=sv, pub_drain=False)
                evk_marks.append(p.val(sv))
                kvw = []
                if t >= 2:
                    kvw.append((sv, kevac[t - 2]))
                p.op("vector", lambda e, t=t: rope_ops(e, t, sc1[:], K2[t % 2]),
                     waits=kvw, inc=sv, pub_drain=False)
                krope_v.append(p.val(sv))
                kqw = [(sv, krope_v[t])]
                if t >= 2:
                    kqw.append((sv, kred_marks[t - 2]))
                p.op("scalar", lambda e, t=t: e.activation(
                    out=kdump[t % 2], in_=K2[t % 2], func=AF.Square),
                    waits=kqw, inc=sa, pub_drain=False)
                ksq_marks.append(p.val(sa))
                if t >= 1:
                    emit_reds(t - 1)
                def emit_ktrq(t=t):
                    def ktrq(e, t=t):
                        ins = None
                        for c in range(NCH):
                            ins = e.transpose(out=ptT[t % 2][:, c * 128:(c + 1) * 128],
                                              in_=K2[t % 2][:, c * 128:(c + 1) * 128],
                                              identity=ident_bf[:])
                        return ins
                    p.op("tensor", ktrq, waits=[(sv, krope_v[t]), (sv, qevac[t])],
                         inc=stt, tdrain=True)
                    m3 = p.val(stt)
                    p.op("scalar", lambda e, t=t: e.copy(
                        out=_r3(featK[:], NCH, t * 128, (t + 1) * 128),
                        in_=_r3(ptT[t % 2][:], NCH)),
                        waits=[(stt, m3)], inc=sv, pub_drain=False)
                    kevac.append(p.val(sv))
                ktrq_pend.append(emit_ktrq)

            tailA.append(emit_tailA)
        while tailA:
            tailA.pop(0)()
        while ktrq_pend:
            ktrq_pend.pop(0)()
        emit_reds(NT - 1)
        ln1_done_sv = lnevac[-1]
        q_done_sv = qevac[-1]
        k_done_sv = kevac[-1]

        def dump_feat(name, src, gate_sem, gate_val, width=None):
            if name not in dbg_h:
                return
            p.op("gpsimd", lambda e: e.dma_start(
                out=dbg_h[name], in_=src[:].bitcast(F32) if hasattr(src, "sbuf_range") or True else src),
                waits=[(gate_sem, gate_val)], inc=sdbg, amount=16)

        dump_feat("ln1f", featA, sv, ln1_done_sv)

        def sq_fin(e, src, dst):
            e.tensor_max(out=src[:, 0:128], in0=src[:, 0:128], in1=src[:, 128:256])
            e.drain()
            e.tensor_max(out=src[:, 0:64], in0=src[:, 0:64], in1=src[:, 64:128])
            e.drain()
            e.tensor_max(out=src[:, 0:32], in0=src[:, 0:32], in1=src[:, 32:64])
            e.drain()
            return e.tensor_max(out=dst[:], in0=src[:, 0:16], in1=src[:, 16:32])
        def qsq_fin(e):
            e.drain()
            sq_fin(e, nrm_allQ, qsqr)
            e.drain()
            return sq_fin(e, nrm_allK, ksqr)
        p.op("vector", qsq_fin, waits=[], inc=sv)
        sq_done_sv = p.val(sv)

        p.op("sync", lambda e: dma(e, bias0[:], bv_h[None, :].partition_broadcast(128)),
             waits=[(sv, qrope_v[NT - 1])], inc=s_ld, amount=16)
        bv_ld = p.val(s_ld)
        p.op("sync", lambda e: dma(e, bias1[:], bf_h[None, :].partition_broadcast(128)),
             waits=[(sv, krope_v[NT - 1])], inc=s_ld, amount=16)
        bf_ld = p.val(s_ld)
        load_weight("wv", wv_h, wbfA, first_waits=[(smm, qproj_mms[NT - 1])])

        v_marks = []
        for t in range(NT):
            waits = [(sv, kevac[t]), (swc, cast_done["wv"])]
            if t >= 2:
                waits.append((sv, v_marks[t - 2]))
            p.op("tensor", proj_matmuls(t, pq[t % 2], wbfA), waits=waits, inc=smm)
            smm_t = p.val(smm)
            def v_ev(e, t=t):
                base = t * 1040
                e.tensor_add(
                    out=vtm[:, base:base + 1040].rearrange("p (h c) -> p h c", c=65)[:, :, 0:64],
                    in0=_r3(pq[t % 2], 16), in1=_r3(bias0[:], 16))
                return e.memset(
                    vtm[:, base:base + 1040].rearrange("p (h c) -> p h c", c=65)[:, :, 64:65], 1.0)
            p.op("vector", v_ev, waits=[(smm, smm_t), (s_ld, bv_ld)], inc=sv, pub_drain=False)
            v_marks.append(p.val(sv))
        v_done_sv = p.val(sv)

        load_weight("wo", wo_h, wbfB)

        dump_feat("qf", featQ, sv, q_done_sv)
        if "qsqr" in dbg_h:
            p.op("gpsimd", lambda e: e.dma_start(out=dbg_h["qsqr"], in_=qsqr[:]),
                 waits=[(sv, sq_done_sv)], inc=sdbg, amount=16)
        if "ksqr" in dbg_h:
            p.op("gpsimd", lambda e: e.dma_start(out=dbg_h["ksqr"], in_=ksqr[:]),
                 waits=[(sv, sq_done_sv)], inc=sdbg, amount=16)
        dump_feat("wbfB", wbfB, sv, k_done_sv)
        dump_feat("wbfA2", wbfA, sv, v_done_sv)
        dump_feat("kf", featK, sv, k_done_sv)
        dump_feat("vtm", vtm, sv, v_done_sv)

        def cb_tp(e):
            e.transpose(out=P0[0:16, 0:128], in_=qsqr[:], identity=ident_f[:])
            return e.transpose(out=P1[0:16, 0:128], in_=ksqr[:], identity=ident_f[:])
        p.op("tensor", cb_tp, waits=[(sv, v_done_sv)], inc=smm, tdrain=True)
        cb_mm = p.val(smm)
        def cb_red(e):
            e.tensor_copy(out=cbtmp, in_=P0[0:16, 0:128])
            e.drain()
            e.reduce_max(out=cbt2, in_=cbtmp, axis=AX.X)
            e.drain()
            e.tensor_copy(out=cbtmp, in_=P1[0:16, 0:128])
            e.drain()
            e.reduce_max(out=cbt4[0:1, 0:1], in_=cbtmp[0:16, :].rearrange("p (u n) -> p u n", u=1),
                         axis=AX.X) if False else e.reduce_max(out=nrm_t[0:16, 0:1], in_=cbtmp, axis=AX.X)
            e.drain()
            return e.tensor_mul(out=cbt2, in0=cbt2, in1=nrm_t[0:16, 0:1])
        p.op("vector", cb_red, waits=[(smm, cb_mm)], inc=sv)
        cb_sv2 = p.val(sv)
        p.op("tensor", lambda e: e.transpose(out=P0[0:1, 0:16], in_=cbt2,
                                             identity=ident_f[0:16, 0:16]),
             waits=[(sv, cb_sv2)], inc=smm, tdrain=True)
        cb_mm2 = p.val(smm)
        def cb_fin(e):
            e.tensor_copy(out=cbt4, in_=P0[0:1, 0:16])
            e.drain()
            return e.reduce_max(out=cbt5, in_=cbt4, axis=AX.X)
        p.op("vector", cb_fin, waits=[(smm, cb_mm2)], inc=sv)
        cb_sv3 = p.val(sv)
        if "cbt2" in dbg_h:
            p.op("gpsimd", lambda e: e.dma_start(out=dbg_h["cbt2"], in_=cbt2),
                 waits=[(sv, cb_sv3)], inc=sdbg, amount=16)
            p.op("gpsimd", lambda e: e.dma_start(out=dbg_h["cbt5pre"], in_=cbt5),
                 waits=[(sv, cb_sv3)], inc=sdbg, amount=16)
        p.op("scalar", lambda e: e.activation(out=cbt5, in_=cbt5, func=AF.Sqrt),
             waits=[(sv, cb_sv3)] + ([(sdbg, p.val(sdbg))] if "cbt2" in dbg_h else []), inc=sa)
        cb_sa = p.val(sa)
        p.op("vector", lambda e: e.tensor_scalar_mul(out=cbt5, in0=cbt5, scalar1=-0.127),
             waits=[(sa, cb_sa)], inc=sv)
        cb_sv4 = p.val(sv)
        p.op("tensor", lambda e: e.matmul(out=P0[0:128, 0:1], lhsT=ones_row_f[:],
                                          rhs=cbt5, start=True, stop=True),
             waits=[(sv, cb_sv4)], inc=smm)
        cb_mm3 = p.val(smm)
        p.op("vector", lambda e: e.tensor_copy(out=negCb[:], in_=P0[0:128, 0:1]),
             waits=[(smm, cb_mm3)], inc=sv)
        negcb_sv = p.val(sv)
        if "negcb" in dbg_h:
            p.op("gpsimd", lambda e: e.dma_start(out=dbg_h["negcb"], in_=negCb[:]),
                 waits=[(sv, negcb_sv)], inc=sdbg, amount=16)

        iters = []
        for j in range(NPAIR):
            for qc in range(NQC):
                jp = (j * NQC + qc) % 2
                ktmax = 4 * qc + 4
                for kt in range(ktmax):
                    dj = kt - 4 * qc
                    qoff = max(0, 128 * dj)
                    iters.append((j, qc, kt, jp, qoff, 512 - qoff, qc * 512 + qoff,
                                  kt == 0, kt == ktmax - 1))
        NIT = len(iters)

        def st_fn_for(i):
            j, qc, kt, jp, qoff, n, qlo, first, last = iters[i]
            pp = i % 2
            def st_fn(e):
                e.matmul(out=pq[pp][:, qoff:qoff + n],
                         lhsT=featK[0:64, j * S_LEN + kt * 128: j * S_LEN + (kt + 1) * 128],
                         rhs=featQ[0:64, j * S_LEN + qlo: j * S_LEN + qlo + n],
                         start=True, stop=True, tile_position=(0, 0))
                return e.matmul(out=pq[pp][:, 512 + qoff:512 + qoff + n],
                                lhsT=featK[64:128, j * S_LEN + kt * 128: j * S_LEN + (kt + 1) * 128],
                                rhs=featQ[64:128, j * S_LEN + qlo: j * S_LEN + qlo + n],
                                start=True, stop=True, tile_position=(64, 0))
            return st_fn

        st_marks = {}
        exp_marks = {}
        pv_marks = {}
        oev_marks = []
        por_marks = []
        evac_pending = None

        def emit_st(i):
            stw = [(sv, negcb_sv)]
            if i >= 2:
                stw.append((sexp, exp_marks[i - 2]))
            p.op("tensor", st_fn_for(i), waits=stw, inc=smm)
            st_marks[i] = p.val(smm)

        def emit_exp_mask(i):
            j, qc, kt, jp, qoff, n, qlo, first, last = iters[i]
            pp = i % 2
            sb3 = i % 3
            expw = [(smm, st_marks[i])]
            if i >= 3:
                expw.append((spv, pv_marks[i - 3]))
            p.op("scalar", lambda e, pp=pp, sb3=sb3: e.activation(
                out=P_sb[sb3][:], in_=pq[pp], func=AF.Exp,
                bias=negCb[:], scale=0.125),
                waits=expw, inc=sexp, pub_drain=False)
            exp_marks[i] = p.val(sexp)
            mask_wait = []
            if kt - 4 * qc >= 0:
                def mk_fn(e, sb3=sb3, qoff=qoff):
                    v = _r3(P_sb[sb3][:], 2, qoff, qoff + 128)
                    tv = _r3(tri2[:], 2)
                    return e.tensor_mul(out=v, in0=v, in1=tv)
                p.op("vector", mk_fn, waits=[(sexp, exp_marks[i])], inc=smsk,
                     pub_drain=False)
                mask_wait = [(smsk, p.val(smsk))]
            return mask_wait

        def emit_pv(i, mask_wait):
            j, qc, kt, jp, qoff, n, qlo, first, last = iters[i]
            sb3 = i % 3
            def pv_fn(e, sb3=sb3):
                e.matmul(out=po_lo[jp][0:128, qoff:qoff + n],
                         lhsT=vtm[:, kt * 1040 + (2 * j) * 65: kt * 1040 + (2 * j) * 65 + 128],
                         rhs=P_sb[sb3][:, qoff:qoff + n],
                         start=first, stop=False)
                return e.matmul(out=po_hi[jp][0:128, qoff:qoff + n],
                                lhsT=vtm[:, kt * 1040 + (2 * j + 1) * 65: kt * 1040 + (2 * j + 1) * 65 + 128],
                                rhs=P_sb[sb3][:, 512 + qoff:512 + qoff + n],
                                start=first, stop=False)
            pvw = [(sexp, exp_marks[i]), (sv, v_done_sv)] + mask_wait
            if first:
                if len(oev_marks) >= 2:
                    pvw.append((sv, oev_marks[-2]))
                if len(por_marks) >= 2:
                    pvw.append((sv, por_marks[-2][0]))
            p.op("tensor", pv_fn, waits=pvw, inc=spv)
            pv_marks[i] = p.val(spv)

        def emit_evac(i):
            j, qc, kt, jp, qoff, n, qlo, first, last = iters[i]
            def oev_fn(e, j=j, qc=qc, jp=jp):
                e.tensor_copy(
                    out=featA[0:64, j * S_LEN + qc * 512: j * S_LEN + qc * 512 + 512],
                    in_=po_lo[jp][0:64, :])
                return e.tensor_copy(
                    out=featA[64:128, j * S_LEN + qc * 512: j * S_LEN + qc * 512 + 512],
                    in_=po_hi[jp][0:64, :])
            p.op("vector", oev_fn, waits=[(spv, pv_marks[i])], inc=sv, pub_drain=False)
            oev_marks.append(p.val(sv))
            rgw = [(spv, pv_marks[i])]
            if len(por_marks) >= 2:
                rgw.append((s_rg, por_marks[-2][1]))
            srb = [sc1[0:1, :], xt[1][0:1, :]][len(por_marks) % 2]
            def stage_fn(e, jp=jp, srb=srb):
                e.tensor_copy(out=srb[:, 0:512], in_=po_lo[jp][64:65, :])
                return e.tensor_copy(out=srb[:, 512:1024], in_=po_hi[jp][64:65, :])
            p.op("vector", stage_fn, waits=rgw, inc=sv)
            stage_sv = p.val(sv)
            p.op("gpsimd", lambda e, j=j, qc=qc, srb=srb: e.dma_start(
                out=rs_all[j:j + 1, qc * 1024:(qc + 1) * 1024], in_=srb),
                waits=[(sv, stage_sv)], inc=s_rg, amount=16)
            por_marks.append((stage_sv, p.val(s_rg)))

        pending_mask = {}
        emit_st(0)
        pending_mask[0] = emit_exp_mask(0)
        for i in range(NIT):
            if i + 1 < NIT:
                emit_st(i + 1)
                pending_mask[i + 1] = emit_exp_mask(i + 1)
            emit_pv(i, pending_mask.pop(i))
            if iters[i][8]:
                emit_evac(i)
        attn_done_sv = p.val(sv)

        p.op("vector", lambda e: e.reciprocal(out=rs_all, in_=rs_all),
             waits=[(sv, attn_done_sv), (s_rg, por_marks[-1][1])], inc=sv)
        rs_rc = p.val(sv)
        p.op("vector", lambda e: e.tensor_copy(out=rs_hi, in_=rs_all),
             waits=[(sv, rs_rc)], inc=sv)
        rs_hi_sv = p.val(sv)
        p.op("vector", lambda e: e.tensor_copy(out=rs_tmp, in_=rs_hi),
             waits=[(sv, rs_hi_sv)], inc=sv)
        rs_t_sv = p.val(sv)
        p.op("vector", lambda e: e.tensor_sub(out=rs_tmp, in0=rs_all, in1=rs_tmp),
             waits=[(sv, rs_t_sv)], inc=sv)
        rs_t2_sv = p.val(sv)
        p.op("vector", lambda e: e.tensor_copy(out=rs_lo, in_=rs_tmp),
             waits=[(sv, rs_t2_sv)], inc=sv)
        rs_sv = p.val(sv)
        p.op("gpsimd", lambda e: mask_init(e, mask8f), waits=[(sv, rs_sv)], inc=s_g)
        mask8f_g = p.val(s_g)
        if "rs" in dbg_h:
            p.op("gpsimd", lambda e: e.dma_start(out=dbg_h["rs"], in_=rs_all),
                 waits=[(sv, rs_sv)], inc=sdbg, amount=16)

        nrm_marks = []
        for j in range(NPAIR):
            for qc in range(NQC):
                jp2 = (j * NQC + qc) % 2
                def ex_fn(e, j=j, qc=qc, jp2=jp2):
                    dst = [P2, P3][jp2]
                    e.matmul(out=dst[0:64, 0:512], lhsT=mask8[:, j * 128: j * 128 + 64],
                             rhs=rs_hi[0:8, qc * 1024: qc * 1024 + 512],
                             start=True, stop=False, tile_position=(0, 0))
                    e.matmul(out=dst[0:64, 0:512], lhsT=mask8[:, j * 128: j * 128 + 64],
                             rhs=rs_lo[0:8, qc * 1024: qc * 1024 + 512],
                             start=False, stop=True, tile_position=(0, 0))
                    e.matmul(out=dst[64:128, 0:512], lhsT=mask8[:, j * 128 + 64: j * 128 + 128],
                             rhs=rs_hi[0:8, qc * 1024 + 512: qc * 1024 + 1024],
                             start=True, stop=False, tile_position=(0, 64))
                    return e.matmul(out=dst[64:128, 0:512], lhsT=mask8[:, j * 128 + 64: j * 128 + 128],
                                    rhs=rs_lo[0:8, qc * 1024 + 512: qc * 1024 + 1024],
                                    start=False, stop=True, tile_position=(0, 64))
                w = [(sv, rs_sv), (s_g, mask8f_g)]
                if len(nrm_marks) >= 2:
                    w.append((sv, nrm_marks[-2]))
                p.op("tensor", ex_fn, waits=w, inc=smm)
                ex_mm = p.val(smm)
                p.op("scalar", lambda e, jp2=jp2: e.copy(out=rb_sb[jp2],
                                                          in_=[P2, P3][jp2][:, 0:512]),
                     waits=[(smm, ex_mm)], inc=sa, pub_drain=False)
                ex_sa = p.val(sa)
                def nm_fn(e, j=j, qc=qc, jp2=jp2):
                    sl = featA[:, j * S_LEN + qc * 512: j * S_LEN + qc * 512 + 512]
                    return e.tensor_mul(out=sl, in0=sl, in1=rb_sb[jp2])
                p.op("vector", nm_fn, waits=[(sa, ex_sa)], inc=sv, pub_drain=False)
                nrm_marks.append(p.val(sv))
        onorm_sv = p.val(sv)
        dump_feat("of", featA, sv, onorm_sv)

        p.op("sync", lambda e: dma(e, bias0[:], bo_h[None, :].partition_broadcast(128)),
             waits=[(sv, v_done_sv)], inc=s_ld, amount=16)
        bo_ld = p.val(s_ld)

        wf_marks = []
        for c in range(NCH):
            p.op("sync", w_dma(wf_h, c), waits=[(swc, wland_free[0])], inc=s_ld, amount=16)
            ld_mark = p.val(s_ld)
            def q1(e, c=c):
                e.tensor_reduce(out=wsc[:, c:c + 1], in_=wland[:], axis=AX.X,
                                op=ALU.max, apply_absolute_value=True)
                e.drain()
                e.tensor_scalar_max(out=wsc[:, c:c + 1], in0=wsc[:, c:c + 1], scalar1=1e-8)
                e.drain()
                e.tensor_scalar_mul(out=wsc[:, c:c + 1], in0=wsc[:, c:c + 1], scalar1=1.0 / 127.0)
                e.drain()
                return e.reciprocal(out=wrxs[:, c:c + 1], in_=wsc[:, c:c + 1])
            p.op("vector", q1, waits=[(s_ld, ld_mark), (sv, attn_done_sv)], inc=sv)
            q1_sv = p.val(sv)
            def q2(e, c=c):
                e.activation(out=wfsc, in_=wland[:], func=AF.Identity,
                             bias=cmag[:], scale=wrxs[:, c:c + 1])
                return e.activation(out=wbfA[:, c * D:(c + 1) * D], in_=wfsc,
                                    func=AF.Identity, bias=cnmag[:], scale=cone[:])
            p.op("scalar", q2, waits=[(sv, q1_sv), (sv, onorm_sv)], inc=swc, pub_drain=False)
            wland_free[0] = p.val(swc)
            wf_marks.append(p.val(swc))
        wfq_done = p.val(swc)

        wt_marks = []
        for c in range(NCH):
            def wt_fn(e, c=c):
                ins = None
                for i in range(NCH):
                    ins = e.transpose(out=ptT[c % 2][:, i * 128:(i + 1) * 128],
                                      in_=wbfA[:, c * D + i * 128: c * D + (i + 1) * 128],
                                      identity=ident_bf[:])
                return ins
            w = [(swc, wfq_done), (sv, onorm_sv)]
            if c >= 2:
                w.append((sv, wt_marks[c - 2]))
            p.op("tensor", wt_fn, waits=w, inc=stt, tdrain=True)
            wt_mm = p.val(stt)
            def wt_ev(e, c=c):
                return e.tensor_copy(
                    out=wqTf.rearrange("p (i d) -> p i d", i=NCH)[:, :, c * 128:(c + 1) * 128],
                    in_=_r3(ptT[c % 2][:], NCH))
            p.op("vector", wt_ev, waits=[(stt, wt_mm)], inc=sv, pub_drain=False)
            wt_marks.append(p.val(sv))
        wqt_done_sv = p.val(sv)

        p.op("tensor", lambda e: e.transpose(out=P2[0:8, 512:640], in_=wsc[:], identity=ident_f[:]),
             waits=[(swc, wfq_done), (sv, wqt_done_sv)], inc=smm, tdrain=True)
        ws_mm = p.val(smm)
        p.op("vector", lambda e: e.tensor_copy(out=wsr[:], in_=P2[0:8, 512:640]),
             waits=[(smm, ws_mm)], inc=sv)
        ws_sv = p.val(sv)
        def wsb_fn(e):
            ins = None
            for c in range(NCH):
                ins = e.matmul(out=pf[:, c * 128:(c + 1) * 128],
                               lhsT=mask8f[:, c * 128:(c + 1) * 128], rhs=wsr[:],
                               start=True, stop=True)
            return ins
        p.op("tensor", wsb_fn, waits=[(sv, ws_sv), (g_set, 1), (s_g, mask8f_g)], inc=smm)
        wsb_mm = p.val(smm)
        p.op("vector", lambda e: e.tensor_copy(out=wsb, in_=pf), waits=[(smm, wsb_mm)], inc=sv,
             pub_drain=False)
        wsb_sv = p.val(sv)

        scq4 = [vtm[:, 12288:13312], vtm[:, 13312:14336],
                vtm[:, 14336:15360], vtm[:, 15360:16384]]
        x2_marks = []
        apply_marks = []
        obuf_marks = []
        xq_sc_marks = []
        gam_marks = []
        rd_marks = []
        trx_marks = []
        ffn_mms = []
        ep_marks = []
        acc_marks = []
        dbg_x2_ld = []
        tailT_queue = []
        tailV_queue = []
        for t in range(NT):
            g, j = divmod(t, 4)
            g4 = g * 4
            xw = []
            if t >= 2:
                xw = [(sv, x2_marks[t - 2])]
            p.op("gpsimd", lambda e, t=t: dma(e, xt[t % 4][:],
                                              x_h.rearrange("(t p) d -> t p d", p=128)[t]),
                 waits=xw, inc=s_x, amount=16)
            x_mark = p.val(s_x)
            def op_fn(e, t=t):
                ins = None
                for n2 in range(2):
                    for c in range(NCH):
                        ins = e.matmul(out=pq[t % 2][:, n2 * 512:(n2 + 1) * 512],
                                       lhsT=featA[:, c * S_LEN + t * 128: c * S_LEN + (t + 1) * 128],
                                       rhs=wbfB[:, c * D + n2 * 512: c * D + n2 * 512 + 512],
                                       start=(c == 0), stop=(c == NCH - 1))
                return ins
            w = [(sv, onorm_sv), (swc, cast_done["wo"]), (sv, wsb_sv)]
            if t >= 2:
                w.append((sv, x2_marks[t - 2]))
            p.op("tensor", op_fn, waits=w, inc=smm)
            op_mm = p.val(smm)
            if t >= 4:
                tailT_queue.pop(0)()
            def x2_fn(e, t=t):
                e.tensor_add(out=sc1[:], in0=pq[t % 2], in1=bias0[:])
                return e.tensor_add(out=x2g4[t % 4], in0=sc1[:], in1=xt[t % 4][:])
            x2w = [(smm, op_mm), (s_x, x_mark), (s_ld, bo_ld), (swc, wfq_done)]
            if t >= 4:
                x2w.append((sa, apply_marks[g - 1]))
                x2w.append((s_g, obuf_marks[t - 4]))
                if dbg_x2_ld:
                    x2w.append((sdbg, dbg_x2_ld[t - 4]))
            p.op("vector", x2_fn, waits=x2w, inc=sv, pub_drain=False)
            x2_marks.append(p.val(sv))
            x2_sv = p.val(sv)
            if "x2" in dbg_h:
                p.op("gpsimd", lambda e, t=t: e.dma_start(
                    out=dbg_h["x2"].rearrange("(t p) d -> t p d", p=128)[t], in_=x2g4[t % 4]),
                    waits=[(sv, x2_sv)], inc=sdbg, amount=16)
                dbg_x2_ld.append(p.val(sdbg))
            gw = [(sv, x2_sv), (s_ld, bf_ld)]
            if t >= 4:
                gw.append((sfin, 16 * (t - 3)))
            p.op("gpsimd", lambda e, t=t: e.tensor_add(
                out=obuf4[t % 4], in0=x2g4[t % 4], in1=bias1[:]), waits=gw, inc=s_g)
            obuf_marks.append(p.val(s_g))
            def accum_fn(e, t=t):
                e.activation(out=dum, in_=x2g4[t % 4], func=AF.Square,
                             accum_out=stats["ssq"][:, t:t + 1])
                return e.activation(out=dum, in_=x2g4[t % 4], func=AF.Identity,
                                    accum_out=stats["ssm"][:, t:t + 1])
            p.op("scalar", accum_fn, waits=[(sv, x2_sv)],
                 inc=(sa if j == 3 else None))
            if j == 3:
                acc_marks.append(p.val(sa))
                m_sl = stats["m"][:, g4:g4 + 4]
                q_sl = stats["rstd"][:, g4:g4 + 4]
                msq_sl = stats["am"][:, g4:g4 + 4]
                var_sl = stats["nmr"][:, g4:g4 + 4]
                def gchain1(e, g4=g4, m_sl=m_sl, q_sl=q_sl, msq_sl=msq_sl, var_sl=var_sl):
                    e.tensor_scalar_mul(out=m_sl, in0=stats["ssm"][:, g4:g4 + 4],
                                        scalar1=1.0 / D)
                    e.tensor_scalar(out=q_sl, in0=stats["ssq"][:, g4:g4 + 4],
                                    scalar1=1.0 / D, scalar2=1e-5,
                                    op0=ALU.mult, op1=ALU.add)
                    e.drain()
                    e.tensor_mul(out=msq_sl, in0=m_sl, in1=m_sl)
                    e.drain()
                    return e.tensor_sub(out=var_sl, in0=q_sl, in1=msq_sl)
                p.op("vector", gchain1, waits=[(sa, acc_marks[-1])], inc=sv)
                gc1 = p.val(sv)
                p.op("scalar", lambda e, g=g, var_sl=var_sl, q_sl=q_sl: e.activation(
                    out=q_sl, in_=var_sl, func=AF.Sqrt),
                    waits=[(sv, gc1)], inc=sa)
                gsq = p.val(sa)
                def gchain2(e, g=g, m_sl=m_sl, q_sl=q_sl, var_sl=var_sl):
                    e.reciprocal(out=q_sl, in_=q_sl)
                    e.drain()
                    return e.scalar_tensor_tensor(out=var_sl, in0=m_sl, scalar=-1.0,
                                                  in1=q_sl, op0=ALU.mult, op1=ALU.mult)
                p.op("vector", gchain2, waits=[(sa, gsq)], inc=sv)
                gc2 = p.val(sv)
                def gapply(e, g=g, g4=g4):
                    ins = None
                    for jj in range(4):
                        ins = e.activation(out=sc2g4[jj], in_=x2g4[jj], func=AF.Identity,
                                           scale=stats["rstd"][:, g4 + jj:g4 + jj + 1],
                                           bias=stats["nmr"][:, g4 + jj:g4 + jj + 1])
                    return ins
                aw = [(sv, gc2)]
                if gam_marks:
                    aw.append((sv, gam_marks[-1]))
                p.op("scalar", gapply, waits=aw, inc=sa, pub_drain=False)
                apply_marks.append(p.val(sa))
                def gam_fn(e, g=g, g4=g4):
                    e.tensor_reduce(
                        out=stats["am"][:, g4:g4 + 4].rearrange("p (t u) -> p t u", u=1),
                        in_=sc2g.rearrange("p (t d) -> p t d", t=4),
                        axis=AX.X, op=ALU.max, apply_absolute_value=True)
                    e.drain()
                    e.tensor_scalar(out=stats["xs"][:, g4:g4 + 4],
                                    in0=stats["am"][:, g4:g4 + 4],
                                    scalar1=1e-8, scalar2=1.0 / 127.0,
                                    op0=ALU.max, op1=ALU.mult)
                    e.drain()
                    return e.reciprocal(out=stats["rxs"][:, g4:g4 + 4],
                                        in_=stats["xs"][:, g4:g4 + 4])
                p.op("vector", gam_fn, waits=[(sa, apply_marks[-1])], inc=sv)
                gam_marks.append(p.val(sv))
                for jj in range(4):
                    tt_ = g4 + jj
                    def rd_fn(e, tt_=tt_, jj=jj):
                        e.activation(out=rdbuf, in_=sc2g4[jj], func=AF.Identity,
                                     scale=stats["rxs"][:, tt_:tt_ + 1], bias=cmag[:])
                        return e.activation(out=scq4[tt_ % 4], in_=rdbuf,
                                            func=AF.Identity, bias=cnmag[:])
                    rw = [(sv, gam_marks[-1])]
                    if tt_ >= 4:
                        rw.append((stt, trx_marks[tt_ - 4]))
                    p.op("scalar", rd_fn, waits=rw, inc=sa, pub_drain=False)
                    rd_marks.append(p.val(sa))

            def emit_tailT(t=t):
                def trx_fn(e, t=t):
                    ins = None
                    for c in range(NCH):
                        ins = e.transpose(out=ptT[t % 2][:, c * 128:(c + 1) * 128],
                                          in_=scq4[t % 4][:, c * 128:(c + 1) * 128],
                                          identity=ident_bf[:])
                    return ins
                w = [(sa, rd_marks[t]), (sv, wqt_done_sv)]
                if t >= 2:
                    w.append((sa, xq_sc_marks[t - 2]))
                p.op("tensor", trx_fn, waits=w, inc=stt, tdrain=True)
                trx_marks.append(p.val(stt))
                xw2 = [(stt, trx_marks[-1])]
                if ffn_mms:
                    xw2.append((smm, ffn_mms[-1]))
                p.op("scalar", lambda e, t=t: e.copy(
                    out=_r3(xqT, NCH), in_=_r3(ptT[t % 2][:], NCH)),
                    waits=xw2, inc=sa, pub_drain=False)
                xq_sc_marks.append(p.val(sa))
                def ffn_fn(e, t=t):
                    ins = None
                    for n2 in range(2):
                        for c in range(NCH):
                            ins = e.matmul(out=pf[:, n2 * 512:(n2 + 1) * 512],
                                           lhsT=xqT[:, c * 128:(c + 1) * 128],
                                           rhs=wqTf[:, c * D + n2 * 512: c * D + n2 * 512 + 512],
                                           start=(c == 0), stop=(c == NCH - 1))
                    return ins
                w = [(sa, xq_sc_marks[-1]), (sv, wsb_sv)]
                if ep_marks:
                    w.append((sv, ep_marks[-1]))
                p.op("tensor", ffn_fn, waits=w, inc=smm)
                ffn_mms.append(p.val(smm))

            def emit_tailV(t=t):
                def ep_fn(e, t=t):
                    e.scalar_tensor_tensor(out=t2buf, in0=pf,
                                           scalar=stats["xs"][:, t:t + 1],
                                           in1=wsb, op0=ALU.mult, op1=ALU.mult)
                    return e.tensor_add(out=obuf4[t % 4], in0=obuf4[t % 4], in1=t2buf)
                p.op("vector", ep_fn,
                     waits=[(smm, ffn_mms[t]), (s_g, obuf_marks[t])],
                     inc=sv, pub_drain=False)
                ep_marks.append(p.val(sv))
                p.op("sync", lambda e, t=t: e.dma_start(
                    out=out_h.rearrange("(t p) d -> t p d", p=128)[t], in_=obuf4[t % 4]),
                    waits=[(sv, ep_marks[-1])], inc=sfin, amount=16)

            if t >= 4:
                tailV_queue.pop(0)()
            tailT_queue.append(emit_tailT)
            tailV_queue.append(emit_tailV)

        while tailT_queue:
            tailT_queue.pop(0)()
            tailV_queue.pop(0)()

        p.op("gpsimd", lambda e: e.memset(ones_bf[:], 1.0),
             waits=[(sfin, 16 * NT), (s_ld, p.val(s_ld)), (sdbg, p.val(sdbg))])

        block = st.enter_context(nc.Block())
        p.emit(block)

    return nc


_PERM = None

def _perm():
    global _PERM
    if _PERM is None:
        idx = []
        for h in range(16):
            idx.extend((h * 64 + np.arange(0, 64, 2)).tolist())
            idx.extend((h * 64 + np.arange(1, 64, 2)).tolist())
        _PERM = np.asarray(idx)
    return _PERM


def _tables():
    inv = (1.0 / 10000.0 ** (np.arange(0, 64, 2, dtype=np.float32) / 64.0)).astype(np.float32)
    pos = np.arange(S_LEN, dtype=np.float32)
    th = (pos[:, None] * inv[None, :]).astype(np.float32)
    cos_t = np.cos(th.astype(np.float64)).astype(np.float32)
    sin_t = np.sin(th.astype(np.float64)).astype(np.float32)
    return cos_t, sin_t


def make_in_maps(x, wq_, bq_, wk_, bk_, wv_, bv_, wo_, bo_, wf_, bf_):
    perm = _perm()
    cos_t, sin_t = _tables()
    c = np.ascontiguousarray
    wq = c(np.asarray(wq_, np.float32)[:, perm])
    bq = c(np.asarray(bq_, np.float32)[perm])
    wk = c(np.asarray(wk_, np.float32)[:, perm])
    bk = c(np.asarray(bk_, np.float32)[perm])
    base = {"wq": wq, "bq": bq, "wk": wk, "bk": bk,
            "wv": c(np.asarray(wv_, np.float32)), "bv": c(np.asarray(bv_, np.float32)),
            "wo": c(np.asarray(wo_, np.float32)), "bo": c(np.asarray(bo_, np.float32)),
            "wf": c(np.asarray(wf_, np.float32)), "bfb": c(np.asarray(bf_, np.float32)),
            "cost": cos_t, "sint": sin_t}
    x = np.asarray(x, np.float32)
    return [dict(base, x=c(x[b])) for b in range(x.shape[0])]


def kernel(L0_attn_x, L0_attn_wq, L0_attn_bq, L0_attn_wk, L0_attn_bk,
           L0_attn_wv, L0_attn_bv, L0_attn_wo, L0_attn_bo,
           L0_ffn_weight, L0_ffn_bias):
    in_maps = make_in_maps(L0_attn_x, L0_attn_wq, L0_attn_bq, L0_attn_wk, L0_attn_bk,
                           L0_attn_wv, L0_attn_bv, L0_attn_wo, L0_attn_bo,
                           L0_ffn_weight, L0_ffn_bias)
    nc = build()
    res = run_bass_kernel_spmd(nc, in_maps, core_ids=list(range(8)))
    return np.stack([np.asarray(r["out"], np.float32) for r in res.results], axis=0)
